# version 12
# baseline (speedup 1.0000x reference)
import contextlib
import numpy as np

import concourse.bass as bass
import concourse.mybir as mybir
from concourse.bass_utils import run_bass_kernel_spmd

F32 = mybir.dt.float32
BF16 = mybir.dt.bfloat16
AF = mybir.ActivationFunctionType
ALU = mybir.AluOpType
AX = mybir.AxisListType

B_, S_, D_ = 2, 8192, 1024
NCORE = 8
SLOC = S_ // NCORE
T = 2 * SLOC
KT = D_ // 128
NH, HD = 16, 64
E_, FF_ = 8, 4096
CAP = (B_ * S_) // E_
EPS = 1e-5
GRID_, DECAY_ = 12, 0.3
WROW = KT * KT * 128 + KT

ENGS = ("sync", "scalar", "vector", "tensor", "gpsimd")


class Prog:
    def __init__(self, nc):
        self.nc = nc
        self.q = {e: [] for e in ENGS}
        self.sems = {}
        self.cnt = {}
        self.es = contextlib.ExitStack()

    def sem(self, name):
        if name not in self.sems:
            self.sems[name] = self.es.enter_context(self.nc.semaphore(name))
            self.cnt[name] = 0
        return self.sems[name]

    def op(self, eng, fn, inc=None, inc_n=1):
        if inc is not None:
            s = self.sem(inc)
            self.cnt[inc] += inc_n
            self.q[eng].append(lambda e, fn=fn, s=s, n=inc_n: fn(e).then_inc(s, n))
            return self.cnt[inc]
        self.q[eng].append(lambda e, fn=fn: fn(e))
        return None

    def dma(self, eng, out, in_, stream, **kw):
        s = self.sem("D_" + stream)
        self.cnt["D_" + stream] += 16
        self.q[eng].append(lambda e, o=out, i=in_, s=s, kw=kw: e.dma_start(out=o, in_=i, **kw).then_inc(s, 16))
        return self.cnt["D_" + stream]

    def dwait(self, eng, stream):
        self.wait(eng, "D_" + stream)

    def wait(self, eng, semname, val=None):
        s = self.sem(semname)
        v = self.cnt[semname] if val is None else val
        if v > 0:
            self.q[eng].append(lambda e, s=s, v=v: e.wait_ge(s, v))

    def at(self, semname):
        self.sem(semname)
        return self.cnt[semname]

    def barrier(self):
        s = self.sem("bar")
        self.cnt["bar"] += len(ENGS)
        tgt = self.cnt["bar"]
        for e in ENGS:
            self.q[e].append(lambda eng, s=s: (eng.drain(), eng.sem_inc(s, 1)))
            self.q[e].append(lambda eng, s=s, t=tgt: eng.wait_ge(s, t))

    def build(self, block):
        for ename in ENGS:
            ops = list(self.q[ename])
            section = getattr(block, ename)

            def body(eng, ops=ops):
                for f in ops:
                    f(eng)
            section(body)


def ts(i, n=128):
    return slice(i * n, (i + 1) * n)


def build_nc(taps=(), upto=None):
    nc = bass.Bass("TRN2", target_bir_lowering=False, debug=False)
    p = Prog(nc)
    taps = set(taps)
    tap_t = {}

    def din(name, shape, dtype=F32):
        return nc.dram_tensor(name, list(shape), dtype, kind="ExternalInput").ap()

    def dout(name, shape, dtype=F32):
        return nc.dram_tensor(name, list(shape), dtype, kind="ExternalOutput").ap()

    def dint(name, shape, dtype=F32, shared=False):
        kw = {"addr_space": "Shared"} if shared else {}
        return nc.dram_tensor(name, list(shape), dtype, **kw).ap()

    def tap(name, shape, dtype=F32):
        if name in taps and name not in tap_t:
            tap_t[name] = dout("tap_" + name, shape, dtype)
        return tap_t.get(name)

    xT = din("xT", (D_, T))
    Wq_t = din("Wq_t", (KT * KT * 128, 128))
    Wk_t = din("Wk_t", (KT * KT * 128, 128))
    Wv_nat = din("Wv_nat", (D_, D_))
    Wo_t = din("Wo_t", (KT * KT * 128, 128))
    bq_c = din("bq_c", (128, KT))
    bk_c = din("bk_c", (128, KT))
    bv_h = din("bv_h", (64, NH))
    bo_c = din("bo_c", (128, KT))
    Win_t = din("Win_t", (24 * KT * 128, 128))
    bin_c = din("bin_c", (128, 24))
    Wout_t = din("Wout_t", (KT * KT * 128, 128))
    bout_c = din("bout_c", (128, KT))
    g1_c = din("g1_c", (128, KT)); be1_c = din("be1_c", (128, KT))
    g2_c = din("g2_c", (128, KT)); be2_c = din("be2_c", (128, KT))
    wg_in = din("wg_in", (D_, E_))
    w1T_t = din("w1T_t", (KT * 32 * 128, 128))
    w2_nat = din("w2_nat", (FF_, D_))
    b1_c = din("b1_c", (128, 32))
    b2_row = din("b2_row", (1, D_))
    maskT = din("maskT", (S_, SLOC), BF16)
    gate_r = din("gate_r", (128, E_))
    moe_sel = din("moe_sel", (16, 2))
    vones = din("vones", (NH * T, 65), BF16)

    outT = dout("outT", (D_, T))

    XR1 = dint("XR1", (D_, T))
    XR2 = dint("XR2", (D_, T))
    ROWS = dint("ROWS", (8, 4096))
    GM = [dint("GM_A", (E_, T)), dint("GM_B", (E_, T))]
    CNTl = [dint("CNTA_l", (2, E_)), dint("CNTB_l", (2, E_))]
    CNTg = [dint("CNTA_g", (16, E_), shared=True), dint("CNTB_g", (16, E_), shared=True)]
    K_loc = dint("K_loc", (D_, T), BF16)
    V_loc = dint("V_loc", (NH * T, 65), BF16)
    K_AGc = [dint("K_AGc%d" % c, (NCORE * 256, T), BF16, shared=True) for c in range(4)]
    V_AGc = [dint("V_AGc%d" % c, (NCORE * 4 * T, 65), BF16, shared=True) for c in range(4)]
    W_loc = dint("W_loc", (WROW, 128), BF16)
    W_AG = dint("W_AG", (NCORE * WROW, 128), BF16, shared=True)
    OT = dint("OT", (D_, T), BF16)

    W_BIG = 51200
    BIG = p.es.enter_context(nc.sbuf_tensor("BIG", [128, W_BIG], F32))
    BIGb = BIG.bitcast(BF16)

    def vf(off, *shape, parts=128):
        n = 1
        for s in shape:
            n *= s
        ap = BIG[0:parts, off:off + n]
        if len(shape) > 1:
            pat = "p (" + " ".join(f"d{i}" for i in range(len(shape))) + ") -> p " + \
                  " ".join(f"d{i}" for i in range(len(shape)))
            ap = ap.rearrange(pat, **{f"d{i}": s for i, s in enumerate(shape)})
        return ap

    def vb(off, *shape, parts=128):
        n = 1
        for s in shape:
            n *= s
        ap = BIGb[0:parts, 2 * off:2 * off + n]
        if len(shape) > 1:
            pat = "p (" + " ".join(f"d{i}" for i in range(len(shape))) + ") -> p " + \
                  " ".join(f"d{i}" for i in range(len(shape)))
            ap = ap.rearrange(pat, **{f"d{i}": s for i, s in enumerate(shape)})
        return ap

    arena_A = vb(0, KT, T)
    xbf = vb(8192, KT, T)
    evtmp = vb(16384, 2, T)
    s1 = vf(18432, T)
    rows_s = vf(20480, 6144, parts=1)
    w2_s = vb(26624, 32, D_)
    w1b = vb(43008, 2, 32, 128)
    wcol = vb(47104, 2, D_)
    crow_s = vf(48128, D_, parts=1)
    crow_b = vb(49152, D_, parts=1)
    b2r_s = vf(49664, D_, parts=1)
    arena_Q = vb(26624, KT, T)
    arena_K = vb(34816, KT, SLOC)
    arena_V1 = vb(38912, KT, SLOC)
    arena_VD = vb(43008, KT, SLOC)
    wbuf = vb(47104, 2, KT, 128)
    prod_s = vb(48128, 2, SLOC)
    pbc2 = vb(49152, 2, SLOC)
    sm4 = vf(34816, T, parts=4)
    sm4b = vf(36864, T, parts=4)
    sqb = vb(38912, 2, KT, 512)
    abc_bc = vf(43008, 2, T)
    lg_row = vf(26624, T, parts=8)
    gmr = vf(28672, T, parts=8)
    glt = vf(30720, 16, E_)
    g_oh = vf(30848, 16, E_)
    g_w = vf(30976, 16, E_)
    g_u = vf(31104, 16, E_)
    fst_b = vf(31232, 16, E_)
    gexp = vf(31360, 16, E_)
    cums = vf(31488, 16, E_)
    g_mx = vf(31616, 16)
    g_sm = vf(31648, 16)
    g_gv = vf(31680, 16)
    g_keep = vf(31712, 16)
    g_gm = vf(31744, 16)
    g_loc = vf(31776, 16)
    offb = vf(31808, 2, E_)
    r_row = vf(31840, E_, parts=1)
    cnt_row = vf(31872, 16, parts=1)
    offs_v = vf(31936, E_, parts=2)
    cg_s = vf(31968, E_, parts=16)
    wmoe_a = vb(16384, E_, KT, 128)
    wmoe_b = vb(20480, E_, KT, 128)
    cmoe = vb(24576, E_ * KT * 128, parts=1)
    gm_ch = vb(28672, E_, 512)
    zbuf = vb(30720, E_, KT, 512)
    att_q = vb(33024, KT, T)
    wv_s = vb(43008, KT, D_)
    kst = vb(0, 2, 2, S_)
    vau = vb(16384, 2, 4, 64 * 65)
    peb = vb(41216, 2, 2, 1024)
    pmb = vb(43264, 2, 2, 1024)
    mkb = vb(45312, 2, 2, 512)
    rcp65 = vf(46336, 2048, parts=65)
    obf = vb(50432, 512, parts=64)
    rcpb = vb(48384, 2048, parts=128)
    pvo2 = vb(49408, 2048, parts=65)
    dsm = vf(50688, 16, parts=128)
    dsm2 = vf(50704, 16, parts=128)
    obf2 = vb(50720, 512, parts=64)
    mvb = vb(46336, 2048, parts=64)
    bvbc = vf(50176, 1024)

    xr_l = vf(41216, 2, T)

    def sbuf(name, shape, dtype=F32):
        return p.es.enter_context(nc.sbuf_tensor(name, list(shape), dtype))

    ones_col = sbuf("ones_col", (128, 1), BF16)
    ones_col_f = sbuf("ones_col_f", (128, 1), F32)
    ones_row_f = sbuf("ones_row_f", (1, 128), F32)
    onesr64 = sbuf("onesr64", (65, 64), F32)
    eps_t = sbuf("eps_t", (1, 1), F32)
    triu_f = sbuf("triu_f", (128, 128), F32)
    id_f = sbuf("id_f", (128, 128), F32)
    gater = sbuf("gater", (128, E_), F32)
    selmat = sbuf("selmat", (16, 2), F32)
    bias = {}
    for nm, w in [("bq", KT), ("bk", KT), ("bo", KT), ("bin", 24), ("bout", KT),
                  ("g1", KT), ("be1", KT), ("g2", KT), ("be2", KT)]:
        bias[nm] = sbuf("c_" + nm, (128, w), F32)
    bvh_s = sbuf("bvh_s", (64, NH), F32)
    b1_s = sbuf("b1_s", (128, 32), BF16)
    wg_s = sbuf("wg_s", (128, KT, E_), BF16)
    hsel_s = sbuf("hsel_s", (128, 4 * KT), BF16)

    psA = p.es.enter_context(nc.psum_tensor("psA", [128, 2048], F32))
    psB = p.es.enter_context(nc.psum_tensor("psB", [128, 2048], F32))

    p.op("gpsimd", lambda e: e.memset(ones_col[:], 1.0))
    p.op("gpsimd", lambda e: e.memset(ones_col_f[:], 1.0))
    p.op("gpsimd", lambda e: e.memset(ones_row_f[:], 1.0))
    p.op("gpsimd", lambda e: e.memset(onesr64[64:65, :], 1.0))
    p.op("gpsimd", lambda e: e.memset(eps_t[:], EPS))
    g = p.op("gpsimd", lambda e: e.memset(triu_f[:], 1.0), inc="pool")
    p.wait("gpsimd", "pool", g)
    p.op("gpsimd", lambda e: e.affine_select(out=triu_f[:], in_=triu_f[:], pattern=[[1, 128]],
                                             compare_op=ALU.is_ge, fill=0.0, base=0, channel_multiplier=-1))
    g = p.op("gpsimd", lambda e: e.memset(id_f[:], 1.0), inc="pool")
    p.wait("gpsimd", "pool", g)
    p.op("gpsimd", lambda e: e.affine_select(out=id_f[:], in_=id_f[:], pattern=[[-1, 128]],
                                             compare_op=ALU.is_equal, fill=0.0, base=0, channel_multiplier=1))
    g = p.op("gpsimd", lambda e: e.memset(hsel_s[:], 0.0), inc="pool")
    p.wait("gpsimd", "pool", g)
    for kt in range(KT):
        col = kt * 4 + kt // 2
        p.op("gpsimd", lambda e, col=col: e.memset(hsel_s[:, col:col + 1], 1.0))
    p.dma("gpsimd", gater[:, :], gate_r[:, :], "ini")
    p.dma("gpsimd", selmat[:, :], moe_sel[:, :], "ini")
    for nm, src in [("bq", bq_c), ("bk", bk_c), ("bo", bo_c), ("bin", bin_c), ("bout", bout_c),
                    ("g1", g1_c), ("be1", be1_c), ("g2", g2_c), ("be2", be2_c)]:
        p.dma("gpsimd", bias[nm][:, :], src[:, :], "ini")
    p.dma("gpsimd", bvh_s[:, :], bv_h[:, :], "ini")
    p.dma("gpsimd", b1_s[:, :], b1_c[:, :], "ini")
    p.dma("gpsimd", b2r_s[0:1, :], b2_row[:, :], "ini")
    p.dma("gpsimd", wg_s[:, :, :], wg_in.rearrange("(kt p) e -> p kt e", p=128), "ini",
          allow_slow_non_contiguous=True)
    p.dma("sync", V_loc[:, :], vones[:, :], "von")
    for _e in ENGS:
        p.dwait(_e, "ini")
        p.dwait(_e, "von")
    p.barrier()

    def _stop(name):
        return upto is not None and _PHASES.index(upto) < _PHASES.index(name)
    _PHASES = ["init", "collapse", "ln0", "ffn1", "res1", "ain", "qkv", "attn", "wo", "lnx2", "ffn2", "final"]
    do_collapse = not _stop("collapse")
    if do_collapse:
      p.dma("gpsimd", w2_s[:, :, :], w2_nat.rearrange("(kf p) n -> p kf n", p=128), "cw2")
      p.dwait("tensor", "cw2")
      wcol_evt = {}
      cmm_evt = {}
      for md in range(KT):
          par = md % 2
          if md >= 2:
              p.wait("gpsimd", "pe", cmm_evt[md - 2])
          d = p.dma("gpsimd", w1b[:, par, :, :],
                    w1T_t[md * 32 * 128:(md + 1) * 32 * 128, :].rearrange("(kf p) m -> p kf m", p=128), "cw1" + str(par))
          if md >= 2:
              p.wait("tensor", "dve", wcol_evt[md - 2])
          p.dwait("tensor", "cw1" + str(par))
          ps = psA if par == 0 else psB
          pe_done = None
          for nch in range(2):
              for kf in range(32):
                  pe_done = p.op("tensor", lambda e, ps=ps, kf=kf, nch=nch, par=par:
                       e.matmul(ps[:, ts(nch, 512)], w1b[:, par, kf, :], w2_s[:, kf, ts(nch, 512)],
                                start=(kf == 0), stop=(kf == 31)),
                       inc="pe" if (nch == 1 and kf == 31) else None)
          cmm_evt[md] = pe_done
          p.wait("vector", "pe", pe_done)
          if md >= 2:
              p.dwait("vector", "wst")
          wcol_evt[md] = p.op("vector", lambda e, ps=ps, par=par:
                              e.tensor_copy(wcol[:, par, :], ps[:, 0:1024]), inc="dve")
          p.wait("scalar", "dve", wcol_evt[md])
          p.dma("scalar",
                W_loc[0:KT * KT * 128, :].rearrange("(dt md p) m -> p md dt m", dt=KT, md=KT)[:, md, :, :],
                wcol[:, par, :].rearrange("p (dt m) -> p dt m", dt=KT), "wst")
      p.wait("tensor", "dve")
      pe_done = None
      for nch in range(2):
          for kf in range(32):
              pe_done = p.op("tensor", lambda e, kf=kf, nch=nch:
                   e.matmul(psA[0:1, ts(nch, 512)], b1_s[:, kf:kf + 1], w2_s[:, kf, ts(nch, 512)],
                            start=(kf == 0), stop=(kf == 31)),
                   inc="pe" if (nch == 1 and kf == 31) else None)
      p.wait("vector", "pe", pe_done)
      v = p.op("vector", lambda e: e.tensor_add(crow_s[0:1, :], psA[0:1, 0:1024], b2r_s[0:1, :]), inc="dve")
      p.wait("vector", "dve", v)
      v = p.op("vector", lambda e: e.tensor_copy(crow_b[0:1, :], crow_s[0:1, :]), inc="dve")
      p.wait("scalar", "dve", v)
      p.dma("scalar", W_loc[KT * KT * 128: WROW, :].rearrange("dt m -> (dt m)").unsqueeze(0),
            crow_b[0:1, :], "wst")
      p.dwait("gpsimd", "wst")
      p.op("gpsimd", lambda e: e.collective_compute(
          "AllGather", ALU.bypass, replica_groups=[list(range(NCORE))],
          ins=[W_loc], outs=[W_AG]), inc="cc")
      CC_W = p.at("cc")
    if not do_collapse:
        CC_W = 0

    def emit_ln(dst, gain=None, beta=None, tapnm=None):
        p.wait("vector", "dve")
        p.wait("vector", "act")
        p.wait("tensor", "act")
        p.wait("tensor", "dve")
        sq_evt, ev_evt = {}, {}
        for ch in range(4):
            par = ch % 2
            if ch >= 2:
                p.wait("vector", "pe", sq_evt[ch - 2])
            v = p.op("vector", lambda e, ch=ch, par=par:
                     e.tensor_mul(sqb[:, par, :, :], xbf[:, :, ts(ch, 512)], xbf[:, :, ts(ch, 512)]), inc="dve")
            if ch >= 2:
                p.wait("tensor", "dve", ev_evt[ch - 2])
            ps = psA if par == 0 else psB
            for kt in range(KT):
                p.op("tensor", lambda e, ps=ps, kt=kt, ch=ch:
                     e.matmul(ps[0:1, 0:512], ones_col[:, :], xbf[:, kt, ts(ch, 512)],
                              start=(kt == 0), stop=(kt == KT - 1)))
            p.wait("tensor", "dve", v)
            for kt in range(KT):
                sq_evt[ch] = p.op("tensor", lambda e, ps=ps, kt=kt, par=par:
                     e.matmul(ps[0:1, 512:1024], ones_col[:, :], sqb[:, par, kt, :],
                              start=(kt == 0), stop=(kt == KT - 1)),
                     inc="pe" if kt == KT - 1 else None)
            p.wait("vector", "pe", sq_evt[ch])
            ev_evt[ch] = p.op("vector", lambda e, ps=ps, ch=ch: e.tensor_copy(
                rows_s[0:1, 0:4096].rearrange("a (h c f) -> a h c f", h=2, c=4)[:, :, ch, :],
                ps[0:1, 0:1024].rearrange("a (h f) -> a h f", h=2)), inc="dve")
        p.wait("vector", "dve")
        v = p.op("vector", lambda e: e.tensor_scalar(rows_s[0:1, 0:4096], rows_s[0:1, 0:4096],
                                                     1.0 / D_, None, op0=ALU.mult), inc="dve")
        p.wait("vector", "dve", v)
        v = p.op("vector", lambda e: e.tensor_mul(rows_s[0:1, 4096:6144], rows_s[0:1, 0:2048],
                                                  rows_s[0:1, 0:2048]), inc="dve")
        p.wait("vector", "dve", v)
        v = p.op("vector", lambda e: e.tensor_sub(rows_s[0:1, 2048:4096], rows_s[0:1, 2048:4096],
                                                  rows_s[0:1, 4096:6144]), inc="dve")
        p.wait("scalar", "dve", v)
        a = p.op("scalar", lambda e: e.activation(out=rows_s[0:1, 4096:6144], in_=rows_s[0:1, 2048:4096],
                                                  func=AF.Ln, bias=eps_t[0:1, 0:1]), inc="act")
        p.wait("scalar", "act", a)
        a = p.op("scalar", lambda e: e.activation(out=rows_s[0:1, 2048:4096], in_=rows_s[0:1, 4096:6144],
                                                  func=AF.Exp, scale=-0.5), inc="act")
        p.wait("vector", "act", a)
        v = p.op("vector", lambda e: e.scalar_tensor_tensor(out=rows_s[0:1, 4096:6144],
                                                            in0=rows_s[0:1, 0:2048], scalar=-1.0,
                                                            in1=rows_s[0:1, 2048:4096],
                                                            op0=ALU.mult, op1=ALU.mult), inc="dve")
        p.wait("gpsimd", "dve", v)
        d = p.dma("gpsimd", ROWS[0:1, :], rows_s[0:1, 2048:6144], "lnr")
        p.dwait("gpsimd", "lnr")
        d = p.dma("gpsimd", abc_bc[:, :, :],
                  ROWS[0:1, :].rearrange("a (b t) -> (a b) t", b=2).partition_broadcast(128), "lna")
        p.dwait("vector", "lna")
        for kt in range(KT):
            v = p.op("vector", lambda e, kt=kt: e.tensor_mul(s1[:, :], xbf[:, kt, :], abc_bc[:, 0, :]), inc="dve")
            p.wait("vector", "dve", v)
            v = p.op("vector", lambda e, kt=kt, dst=dst: e.tensor_add(dst[:, kt, :], s1[:, :], abc_bc[:, 1, :]), inc="dve")
            p.wait("vector", "dve", v)
            if gain is not None:
                v = p.op("vector", lambda e, kt=kt, dst=dst, gain=gain, beta=beta: e.tensor_scalar(
                    dst[:, kt, :], dst[:, kt, :], gain[:, kt:kt + 1], beta[:, kt:kt + 1],
                    op0=ALU.mult, op1=ALU.add), inc="dve")
                p.wait("vector", "dve", v)
        if tapnm is not None:
            tp = tap(tapnm, (D_, T), BF16)
            if tp is not None:
                p.wait("gpsimd", "dve")
                p.dma("gpsimd", tp[:, :].rearrange("(kt p) t -> p kt t", p=128), dst[:, :, :], "tap")
                p.dwait("vector", "tap")
                p.dwait("scalar", "tap")
                p.dwait("tensor", "tap")

    def emit_proj(wtile, nd, rhs_arena, evac_fn):
        evt = {}
        pevt = {}
        p.wait("gpsimd", "pe")
        p.wait("tensor", "act")
        p.wait("tensor", "dve")
        for dt in range(nd):
            par = dt % 2
            if dt >= 2:
                p.wait("gpsimd", "pe", pevt[dt - 2])
            d = p.dma("gpsimd", wbuf[:, par, :, :],
                      wtile[dt * KT * 128:(dt + 1) * KT * 128, :].rearrange("(kt p) m -> p kt m", p=128), "wb" + str(par))
            if dt >= 2:
                p.wait("tensor", "act", evt[dt - 2])
            p.dwait("tensor", "wb" + str(par))
            ps = psA if par == 0 else psB
            mmdone = None
            for nch in range(4):
                for kt in range(KT):
                    mmdone = p.op("tensor", lambda e, ps=ps, kt=kt, nch=nch, par=par:
                         e.matmul(ps[:, ts(nch, 512)], wbuf[:, par, kt, :], rhs_arena[:, kt, ts(nch, 512)],
                                  start=(kt == 0), stop=(kt == KT - 1)),
                         inc="pe" if (nch == 3 and kt == KT - 1) else None)
            pevt[dt] = mmdone
            p.wait("scalar", "pe", mmdone)
            evt[dt] = evac_fn(dt, ps, par)
        return evt

    def emit_ffn(idx, tappfx=""):
        def qkv_evac(dt, ps, par):
            if dt < 8:
                return p.op("scalar", lambda e, ps=ps, dt=dt: e.activation(
                    out=arena_Q[:, dt, :], in_=ps[:, :], func=AF.Identity, bias=bias["bin"][:, dt:dt + 1]), inc="act")
            a = p.op("scalar", lambda e, ps=ps, par=par: e.activation(
                out=evtmp[:, par, :], in_=ps[:, :], func=AF.Copy), inc="act")
            p.wait("vector", "act", a)
            if dt < 16:
                kt8 = dt - 8
                v = p.op("vector", lambda e, par=par, kt8=kt8: e.tensor_sub(
                    arena_K[:, kt8, :], evtmp[:, par, 0:SLOC], evtmp[:, par, SLOC:T]), inc="dve")
            else:
                kt8 = dt - 16
                v = p.op("vector", lambda e, par=par, kt8=kt8: e.tensor_sub(
                    arena_VD[:, kt8, :], evtmp[:, par, 0:SLOC], evtmp[:, par, SLOC:T]), inc="dve")
                p.wait("vector", "dve", v)
                v = p.op("vector", lambda e, par=par, kt8=kt8, dt=dt: e.tensor_scalar(
                    arena_V1[:, kt8, :], evtmp[:, par, SLOC:T], bias["bin"][:, dt:dt + 1], None,
                    op0=ALU.add), inc="dve")
            p.wait("scalar", "dve", v)
            return a
        emit_proj(Win_t, 24, arena_A, qkv_evac)

        p.wait("vector", "act")
        p.wait("vector", "dve")
        p.wait("tensor", "act")
        p.wait("tensor", "dve")
        prod_evt = {}
        for l in range(2):
            ps = psA if l == 0 else psB
            for kt in range(KT):
                par = kt % 2
                g8 = l * KT + kt
                if g8 >= 2:
                    p.wait("vector", "pe", prod_evt[g8 - 2])
                v = p.op("vector", lambda e, l=l, kt=kt, par=par: e.tensor_mul(
                    prod_s[:, par, :], arena_Q[:, kt, ts(l, SLOC)], arena_K[:, kt, :]), inc="dve")
                p.wait("tensor", "dve", v)
                for nch in range(2):
                    prod_evt[g8] = p.op("tensor", lambda e, ps=ps, kt=kt, nch=nch, par=par:
                         e.matmul(ps[0:4, ts(nch, 512)], hsel_s[:, kt * 4:(kt + 1) * 4],
                                  prod_s[:, par, ts(nch, 512)], start=(kt == 0), stop=(kt == KT - 1)),
                         inc="pe" if nch == 1 else None)
        p.wait("scalar", "pe")
        for l in range(2):
            ps = psA if l == 0 else psB
            p.op("scalar", lambda e, ps=ps, l=l: e.activation(
                out=sm4[0:4, ts(l, SLOC)], in_=ps[0:4, 0:SLOC], func=AF.Exp, scale=-1.0 / 16.0), inc="act")
        p.wait("vector", "act")
        v = p.op("vector", lambda e: e.tensor_scalar(sm4b[0:4, :], sm4[0:4, :], 1.0, None, op0=ALU.add), inc="dve")
        p.wait("vector", "dve", v)
        v = p.op("vector", lambda e: e.reciprocal(sm4[0:4, :], sm4b[0:4, :]), inc="dve")
        p.wait("gpsimd", "dve", v)
        d = p.dma("gpsimd", ROWS[0:4, 0:T], sm4[0:4, :], "lnr")
        p.dwait("gpsimd", "lnr")
        for l in range(2):
            pdma = {0: p.dma("gpsimd", pbc2[:, 0, :], ROWS[0:1, ts(l, SLOC)].partition_broadcast(128), "pb0")}
            for kt in range(KT):
                if kt + 1 < KT:
                    h2 = (kt + 1) // 2
                    pdma[kt + 1] = p.dma("gpsimd", pbc2[:, (kt + 1) % 2, :],
                                         ROWS[h2:h2 + 1, ts(l, SLOC)].partition_broadcast(128), "pb" + str((kt + 1) % 2))
                p.dwait("vector", "pb" + str(kt % 2))
                v = p.op("vector", lambda e, kt=kt: e.tensor_mul(
                    prod_s[:, 0, :], pbc2[:, kt % 2, :], arena_VD[:, kt, :]), inc="dve")
                p.wait("vector", "dve", v)
                v = p.op("vector", lambda e, kt=kt, l=l: e.tensor_add(
                    arena_Q[:, kt, ts(l, SLOC)], prod_s[:, 0, :], arena_V1[:, kt, :]), inc="dve")
                p.wait("vector", "dve", v)
                p.wait("gpsimd", "dve", v)
        def op_evac(dt, ps, par):
            a = p.op("scalar", lambda e, ps=ps, dt=dt, par=par: e.activation(
                out=evtmp[:, par, :], in_=ps[:, :], func=AF.Identity, bias=bias["bout"][:, dt:dt + 1]), inc="act")
            p.wait("vector", "act", a)
            v = p.op("vector", lambda e, dt=dt, par=par: e.tensor_add(
                xbf[:, dt, :], evtmp[:, par, :], arena_A[:, dt, :]), inc="dve")
            p.wait("scalar", "dve", v)
            return a
        p.wait("tensor", "dve")
        emit_proj(Wout_t, KT, arena_Q, op_evac)
        if tappfx + "y1pre" in taps:
            tp = tap(tappfx + "y1pre", (D_, T), BF16)
            p.wait("gpsimd", "dve")
            p.dma("gpsimd", tp[:, :].rearrange("(kt p) t -> p kt t", p=128), xbf[:, :, :], "tap")
            p.dwait("vector", "tap")
            p.dwait("scalar", "tap")
        emit_ln(arena_A, gain=bias["g1"], beta=bias["be1"],
                tapnm=(tappfx + "y1") if (tappfx + "y1") in taps else None)

        p.wait("tensor", "dve")
        mmdone = None
        for nch in range(4):
            for kt in range(KT):
                mmdone = p.op("tensor", lambda e, kt=kt, nch=nch:
                     e.matmul(psA[0:8, ts(nch, 512)], wg_s[:, kt, :], arena_A[:, kt, ts(nch, 512)],
                              start=(kt == 0), stop=(kt == KT - 1)),
                     inc="pe" if (nch == 3 and kt == KT - 1) else None)
        p.wait("vector", "pe", mmdone)
        v = p.op("vector", lambda e: e.tensor_copy(lg_row[0:8, :], psA[0:8, 0:T]), inc="dve")
        p.wait("tensor", "dve", v)
        for tt in range(16):
            mmdone = p.op("tensor", lambda e, tt=tt: e.transpose(psB[:, tt * 8:(tt + 1) * 8],
                                                        lg_row[0:8, ts(tt, 128)], id_f[0:8, 0:8]),
                          inc="pe" if tt == 15 else None)
        p.wait("vector", "pe", mmdone)
        v = p.op("vector", lambda e: e.tensor_copy(
            glt[:, :, :], psB[:, 0:128].rearrange("p (tt e) -> p tt e", e=8)), inc="dve")
        p.wait("vector", "dve", v)
        v = p.op("vector", lambda e: e.reduce_max(g_mx[:, :], glt[:, :, :], axis=AX.X), inc="dve")
        p.wait("vector", "dve", v)
        v = p.op("vector", lambda e: e.tensor_tensor(
            g_oh[:, :, :], glt[:, :, :], g_mx[:, :].unsqueeze(2).broadcast_to([128, 16, 8]),
            op=ALU.is_ge), inc="dve")
        p.wait("vector", "dve", v)
        v = p.op("vector", lambda e: e.tensor_tensor(
            g_w[:, :, :], g_oh[:, :, :], gater[:, :].unsqueeze(1).broadcast_to([128, 16, 8]),
            op=ALU.mult), inc="dve")
        p.wait("vector", "dve", v)
        v = p.op("vector", lambda e: e.reduce_max(g_sm[:, :], g_w[:, :, :], axis=AX.X), inc="dve")
        p.wait("vector", "dve", v)
        v = p.op("vector", lambda e: e.tensor_tensor(
            fst_b[:, :, :], g_w[:, :, :], g_sm[:, :].unsqueeze(2).broadcast_to([128, 16, 8]),
            op=ALU.is_ge), inc="dve")
        p.wait("vector", "dve", v)
        v = p.op("vector", lambda e: e.tensor_tensor(
            gexp[:, :, :], glt[:, :, :], g_mx[:, :].unsqueeze(2).broadcast_to([128, 16, 8]),
            op=ALU.subtract), inc="dve")
        p.wait("scalar", "dve", v)
        a = p.op("scalar", lambda e: e.activation(out=gexp[:, :, :], in_=gexp[:, :, :], func=AF.Exp), inc="act")
        p.wait("vector", "act", a)
        v = p.op("vector", lambda e: e.reduce_sum(g_sm[:, :], gexp[:, :, :], axis=AX.X), inc="dve")
        p.wait("vector", "dve", v)
        v = p.op("vector", lambda e: e.reciprocal(g_gv[:, :], g_sm[:, :]), inc="dve")
        p.wait("tensor", "dve", v)
        cum_evt = {}
        for seg in range(2):
            rz = p.op("vector", lambda e: e.memset(r_row[0:1, :], 0.0), inc="dve")
            p.wait("tensor", "dve", rz)
            for i in range(8):
                tt = seg * 8 + i
                pcs = slice(1024 + (tt % 2) * 512, 1024 + (tt % 2) * 512 + 8)
                ptt = slice(1024 + (tt % 2) * 512, 1024 + (tt % 2) * 512 + 8)
                if tt >= 2:
                    p.wait("tensor", "dve", cum_evt[tt - 2])
                p.op("tensor", lambda e, pcs=pcs, tt=tt, i=i: e.matmul(
                    psA[:, pcs], triu_f[:, :], fst_b[:, tt, :], start=True, stop=(i == 0)))
                if i > 0:
                    p.op("tensor", lambda e, pcs=pcs: e.matmul(
                        psA[:, pcs], ones_row_f[0:1, :], r_row[0:1, :], start=False, stop=True))
                tot = p.op("tensor", lambda e, ptt=ptt, tt=tt: e.matmul(
                    psB[0:1, ptt], ones_col_f[:, :], fst_b[:, tt, :],
                    start=True, stop=True), inc="pe")
                p.wait("vector", "pe", tot)
                v = p.op("vector", lambda e, pcs=pcs, tt=tt: e.tensor_copy(cums[:, tt, :], psA[:, pcs]), inc="dve")
                p.wait("vector", "dve", v)
                cum_evt[tt] = p.op("vector", lambda e, ptt=ptt: e.tensor_tensor(
                    r_row[0:1, :], r_row[0:1, :], psB[0:1, ptt], op=ALU.add), inc="dve")
                p.wait("vector", "dve", cum_evt[tt])
                p.wait("tensor", "dve", cum_evt[tt])
            v = p.op("vector", lambda e, seg=seg: e.tensor_copy(
                cnt_row[0:1, seg * 8:(seg + 1) * 8], r_row[0:1, :]), inc="dve")
            p.wait("vector", "dve", v)
        p.wait("gpsimd", "dve", v)
        d = p.dma("gpsimd", CNTl[idx][:, :].rearrange("s e -> (s e)").unsqueeze(0), cnt_row[0:1, :], "cnt")
        p.dwait("gpsimd", "cnt")
        p.op("gpsimd", lambda e: e.collective_compute(
            "AllGather", ALU.bypass, replica_groups=[list(range(NCORE))],
            ins=[CNTl[idx]], outs=[CNTg[idx]]), inc="cc")
        p.wait("gpsimd", "cc")
        d = p.dma("gpsimd", cg_s[:, :], CNTg[idx][:, :], "cgl")
        p.dwait("tensor", "cgl")
        mmdone = p.op("tensor", lambda e: e.matmul(psB[0:2, 0:8], selmat[:, :], cg_s[:, :],
                                                   start=True, stop=True), inc="pe")
        p.wait("vector", "pe", mmdone)
        v = p.op("vector", lambda e: e.tensor_copy(offs_v[:, :], psB[0:2, 0:8]), inc="dve")
        p.wait("gpsimd", "dve", v)
        d = p.dma("gpsimd", ROWS[4:6, 0:8], offs_v[:, :], "lnr")
        p.dwait("gpsimd", "lnr")
        for seg in range(2):
            p.dma("gpsimd", offb[:, seg, :], ROWS[4 + seg:5 + seg, 0:8].partition_broadcast(128), "lna")
        p.dwait("vector", "lna")
        for seg in range(2):
            v = p.op("vector", lambda e, seg=seg: e.tensor_tensor(
                g_u[:, ts(seg, 8), :], cums[:, ts(seg, 8), :],
                offb[:, seg:seg + 1, :].broadcast_to([128, 8, 8]), op=ALU.add), inc="dve")
        p.wait("vector", "dve")
        v = p.op("vector", lambda e: e.tensor_tensor(g_u[:, :, :], g_u[:, :, :], fst_b[:, :, :],
                                                     op=ALU.mult), inc="dve")
        p.wait("vector", "dve", v)
        v = p.op("vector", lambda e: e.reduce_sum(g_loc[:, :], g_u[:, :, :], axis=AX.X), inc="dve")
        p.wait("vector", "dve", v)
        if tappfx + "dbg" in taps:
            tdb = tap(tappfx + "dbg", (130, 16))
            p.wait("gpsimd", "dve")
            p.dma("gpsimd", tdb[0:128, :], g_loc[:, :], "tap")
            p.dma("gpsimd", tdb[128:129, :], cnt_row[0:1, 0:16], "tap")
            p.dwait("gpsimd", "tap")
        if tappfx + "cg" in taps:
            tcg = tap(tappfx + "cg", (16, 8))
            p.wait("gpsimd", "dve")
            p.dma("gpsimd", tcg[:, :], cg_s[:, :], "tap")
            p.dwait("gpsimd", "tap")
        v = p.op("vector", lambda e: e.tensor_scalar(g_keep[:, :], g_loc[:, :], CAP + 0.5, None,
                                                     op0=ALU.is_lt), inc="dve")
        p.wait("vector", "dve", v)
        v = p.op("vector", lambda e: e.tensor_mul(g_gm[:, :], g_keep[:, :], g_gv[:, :]), inc="dve")
        p.wait("vector", "dve", v)
        v = p.op("vector", lambda e: e.tensor_tensor(
            g_w[:, :, :], fst_b[:, :, :], g_gm[:, :].unsqueeze(2).broadcast_to([128, 16, 8]),
            op=ALU.mult), inc="dve")
        p.wait("tensor", "dve", v)
        gmr_evt = {}
        for tt in range(16):
            par = tt % 2
            if tt >= 2:
                p.wait("tensor", "dve", gmr_evt[tt - 2])
            mmdone = p.op("tensor", lambda e, tt=tt, par=par: e.transpose(
                psA[0:8, 1024 + par * 512: 1024 + par * 512 + 128], g_w[:, tt, :], id_f[:, :]), inc="pe")
            p.wait("vector", "pe", mmdone)
            gmr_evt[tt] = p.op("vector", lambda e, tt=tt, par=par: e.tensor_copy(
                gmr[0:8, ts(tt, 128)], psA[0:8, 1024 + par * 512: 1024 + par * 512 + 128]), inc="dve")
        p.wait("gpsimd", "dve")
        d = p.dma("gpsimd", GM[idx][:, :], gmr[0:8, :], "gms")
        p.dwait("gpsimd", "gms")
        p.wait("gpsimd", "cc", CC_W)
        for e_ in range(E_):
            p.dma("gpsimd", cmoe[0:1, e_ * 1024:(e_ + 1) * 1024],
                  W_AG[e_ * WROW + KT * KT * 128: e_ * WROW + KT * KT * 128 + KT, :]
                  .rearrange("dt m -> (dt m)").unsqueeze(0), "cmo")
        if tappfx + "gm" in taps:
            tpg = tap(tappfx + "gm", (E_, T))
            p.dma("gpsimd", tpg[:, :], GM[idx][:, :], "tap")
            p.dwait("gpsimd", "tap")
        p.dwait("tensor", "cmo")

        p.wait("sync", "cc", CC_W)
        p.wait("sync", "dve")
        p.wait("sync", "act")
        p.wait("vector", "pe")
        p.wait("tensor", "act")
        p.wait("tensor", "dve")
        moe_evt = {}
        mm_evt = {}
        zprev = None
        for ch in range(4):
            zdone = None
            if ch > 0:
                p.wait("vector", "pe", mm_evt[ch * KT - 1])
                p.wait("gpsimd", "pe", mm_evt[ch * KT - 1])
                p.wait("gpsimd", "dve", zprev)
            gmd = None
            for e_ in range(E_):
                gmd = p.dma("gpsimd", gm_ch[:, e_, :],
                            GM[idx][e_:e_ + 1, ts(ch, 512)].partition_broadcast(128), "gmc")
            p.dwait("vector", "gmc")
            for e_ in range(E_):
                zdone = p.op("vector", lambda e, e_=e_, ch=ch: e.tensor_tensor(
                    zbuf[:, e_, :, :], arena_A[:, :, ts(ch, 512)],
                    gm_ch[:, e_, :].unsqueeze(1).broadcast_to([128, KT, 512]),
                    op=ALU.mult), inc="dve")
            zprev = zdone
            p.wait("tensor", "dve", zdone)
            p.dwait("tensor", "gmc")
            for dt in range(KT):
                par = dt % 2
                gidx = ch * KT + dt
                wm = wmoe_a if par == 0 else wmoe_b
                if gidx >= 2:
                    p.wait("sync", "pe", mm_evt[gidx - 2])
                for e8 in range(NCORE):
                    d = p.dma("sync", wm[:, e8, :, :],
                              W_AG[e8 * WROW + dt * KT * 128: e8 * WROW + (dt + 1) * KT * 128, :]
                              .rearrange("(kt p) m -> p kt m", p=128), "wm" + str(par))
                p.dwait("tensor", "wm" + str(par))
                if gidx >= 2:
                    p.wait("tensor", "dve", moe_evt[gidx - 2])
                ps = psA if par == 0 else psB
                for e_ in range(E_):
                    for kt in range(KT):
                        p.op("tensor", lambda e, ps=ps, wm=wm, e_=e_, kt=kt:
                             e.matmul(ps[:, 0:512], wm[:, e_, kt, :], zbuf[:, e_, kt, :],
                                      start=(e_ == 0 and kt == 0), stop=False))
                    mmdone = p.op("tensor", lambda e, ps=ps, e_=e_, dt=dt:
                         e.matmul(ps[:, 0:512], cmoe[0:1, (e_ * KT + dt) * 128:(e_ * KT + dt) * 128 + 128],
                                  gm_ch[0:1, e_, :],
                                  start=False, stop=(e_ == E_ - 1)),
                         inc="pe" if e_ == E_ - 1 else None)
                mm_evt[gidx] = mmdone
                p.wait("vector", "pe", mmdone)
                moe_evt[gidx] = p.op("vector", lambda e, ps=ps, dt=dt, ch=ch: e.tensor_add(
                    xbf[:, dt, ts(ch, 512)], ps[:, 0:512], arena_A[:, dt, ts(ch, 512)]), inc="dve")
        if tappfx + "y2pre" in taps:
            tp = tap(tappfx + "y2pre", (D_, T), BF16)
            p.wait("gpsimd", "dve")
            p.dma("gpsimd", tp[:, :].rearrange("(kt p) t -> p kt t", p=128), xbf[:, :, :], "tap")
            p.dwait("vector", "tap")
            p.dwait("scalar", "tap")
        emit_ln(arena_A, gain=bias["g2"], beta=bias["be2"])

    def emit_residual(src_dram, dst_dram, scale, tapnm=None):
        use_evt, st_evt, cp_evt = {}, {}, {}
        p.dwait("scalar", "x12")
        for kt in range(KT):
            par = kt % 2
            if kt >= 2:
                p.wait("scalar", "dve", use_evt[kt - 2])
            d = p.dma("scalar", xr_l[:, par, :], src_dram[ts(kt, 128), :], "xr" + str(par))
            p.dwait("vector", "xr" + str(par))
            if kt >= 1:
                p.dwait("vector", "x12")
                p.wait("vector", "act", cp_evt[kt - 1])
            use_evt[kt] = p.op("vector", lambda e, kt=kt, par=par, scale=scale: e.scalar_tensor_tensor(
                out=s1[:, :], in0=arena_A[:, kt, :], scalar=scale, in1=xr_l[:, par, :],
                op0=ALU.mult, op1=ALU.add), inc="dve")
            p.wait("sync", "dve", use_evt[kt])
            p.wait("scalar", "dve", use_evt[kt])
            st_evt[kt] = p.dma("sync", dst_dram[ts(kt, 128), :], s1[:, :], "x12")
            cp_evt[kt] = p.op("scalar", lambda e, kt=kt: e.activation(
                out=xbf[:, kt, :], in_=s1[:, :], func=AF.Copy), inc="act")
        p.dwait("vector", "x12")
        if tapnm is not None:
            tp = tap(tapnm, (D_, T))
            if tp is not None:
                p.dwait("gpsimd", "x12")
                p.dma("gpsimd", tp[:, :], dst_dram[:, :], "tap")
                p.dwait("scalar", "tap")
                p.dwait("vector", "tap")
                p.dwait("sync", "tap")

    if not _stop("ln0"):
        d = p.dma("gpsimd", xbf[:, :, :], xT.rearrange("(kt p) t -> p kt t", p=128), "xb0")
        p.dwait("vector", "xb0")
        p.dwait("tensor", "xb0")
        p.barrier()
        emit_ln(arena_A, tapnm="xin0")
        p.barrier()
    if not _stop("ffn1"):
        emit_ffn(0, tappfx="f1_")
        p.barrier()
    if not _stop("res1"):
        emit_residual(xT, XR1, 0.5, tapnm="x1")
        p.barrier()
    if not _stop("ain"):
        emit_ln(arena_A, tapnm="ain")
        p.barrier()

    do_qkv = not _stop("qkv")
    do_attn = not _stop("attn")
    do_wo = not _stop("wo")
    WV_D = p.dma("gpsimd", wv_s[:, :, :], Wv_nat.rearrange("(kt p) n -> p kt n", p=128), "wv") if do_qkv else 0

    def q_evac(dt, ps, par):
        return p.op("scalar", lambda e, ps=ps, dt=dt: e.activation(
            out=att_q[:, dt, :], in_=ps[:, :], func=AF.Identity, bias=bias["bq"][:, dt:dt + 1],
            scale=0.125), inc="act")
    if do_qkv:
        emit_proj(Wq_t, KT, arena_A, q_evac)

    def k_evac(dt, ps, par):
        a = p.op("scalar", lambda e, ps=ps, dt=dt, par=par: e.activation(
            out=evtmp[:, par, :], in_=ps[:, :], func=AF.Identity, bias=bias["bk"][:, dt:dt + 1]), inc="act")
        p.wait("sync", "act", a)
        p.dma("sync", K_loc[ts(dt, 128), :], evtmp[:, par, :], "kls")
        p.dwait("scalar", "kls")
        return a
    if do_qkv:
      emit_proj(Wk_t, KT, arena_A, k_evac)

      p.dwait("tensor", "wv")
      p.wait("tensor", "act")
      p.wait("tensor", "dve")
      p.wait("gpsimd", "pe", p.at("pe"))
      p.dma("gpsimd", ROWS[6:7, 0:1024].rearrange("r (h v) -> (r h) v", v=64),
            bv_h.rearrange("v h -> h v"), "bvb", allow_slow_non_contiguous=True)
      p.dwait("gpsimd", "bvb")
      p.dma("gpsimd", bvbc[:, :], ROWS[6:7, 0:1024].partition_broadcast(128), "bvb")
      p.dwait("vector", "bvb")
      p.dwait("gpsimd", "kls")
      for c in range(4):
          p.op("gpsimd", lambda e, c=c: e.collective_compute(
              "AllGather", ALU.bypass, replica_groups=[list(range(NCORE))],
              ins=[K_loc[c * 256:(c + 1) * 256, :]], outs=[K_AGc[c]]), inc="cc")
      p.dwait("vector", "kls")
      v_evt = {}
      vcnt = 0
      for tt in range(16):
          for nch in range(2):
              par = vcnt % 2
              if vcnt >= 2:
                  p.dwait("tensor", "vls")
              ps = psA if par == 0 else psB
              for kt in range(KT):
                  mmdone = p.op("tensor", lambda e, ps=ps, tt=tt, kt=kt, nch=nch:
                       e.matmul(ps[:, 0:512], arena_A[:, kt, ts(tt, 128)], wv_s[:, kt, ts(nch, 512)],
                                start=(kt == 0), stop=(kt == KT - 1)),
                       inc="pe" if kt == KT - 1 else None)
              p.wait("vector", "pe", mmdone)
              a = p.op("vector", lambda e, ps=ps, par=par, nch=nch: e.tensor_add(
                  evtmp[:, par, 0:512], ps[:, 0:512], bvbc[:, ts(nch, 512)]), inc="dve")
              p.wait("sync", "dve", a)
              for hh in range(8):
                  h = nch * 8 + hh
                  v_evt[vcnt] = p.dma("sync", V_loc[h * T + tt * 128: h * T + tt * 128 + 128, 0:64],
                                      evtmp[:, par, hh * 64:(hh + 1) * 64], "vls")
              vcnt += 1
      p.dwait("gpsimd", "vls")
      cc_chunk = {}
      for c in range(4):
          p.op("gpsimd", lambda e, c=c: e.collective_compute(
              "AllGather", ALU.bypass, replica_groups=[list(range(NCORE))],
              ins=[V_loc[c * 4 * T:(c + 1) * 4 * T, :]], outs=[V_AGc[c]]), inc="cc")
          cc_chunk[c] = p.at("cc")
    if not do_qkv:
        cc_chunk = {c: 0 for c in range(4)}
    p.barrier()

    ev_mm, ev_exp, ev_tt, ev_pv, ev_nrm = {}, {}, {}, {}, {}
    nrm_fin = {}
    pending_nrm = []
    pending_b = []
    pending_c = []

    ev_rcp = {}
    ot_last = {}

    nrm_prev = [None]

    def emit_nrm_a(jp, qc):
        p.wait("gpsimd", "dve", ev_nrm[(jp, qc)])
        p.dma("gpsimd", ROWS[6:7, 0:2048], pvo2[64:65, :], "nr")
        prevr = ev_rcp.get((jp, qc - 1)) if qc > 0 else (ev_rcp.get((jp - 1, 1)) if jp > 0 else None)
        if prevr is not None:
            p.wait("sync", "dve", prevr)
        p.dwait("sync", "nr")
        p.dma("sync", dsm[:, :], ROWS[6:7, 0:2048].rearrange("r (p f) -> (r p) f", p=128), "nr")

    def emit_nrm_b(jp, qc):
        p.dwait("vector", "nr")
        evr = p.op("vector", lambda e: e.reciprocal(dsm2[:, :], dsm[:, :]), inc="dve")
        ev_rcp[(jp, qc)] = evr
        p.wait("sync", "dve", evr)
        p.dma("sync", ROWS[7:8, 0:2048].rearrange("r (p f) -> (r p) f", p=128), dsm2[:, :], "nr")
        if nrm_prev[0] is not None:
            p.wait("gpsimd", "pool", nrm_fin[nrm_prev[0]])
        p.dwait("gpsimd", "nr")
        p.dma("gpsimd", rcpb[:, :], ROWS[7:8, 0:2048].partition_broadcast(128), "nb")

    def emit_nrm_c(jp, qc):
        p.dwait("gpsimd", "nb")
        p.dwait("gpsimd", "ot0")
        mv = p.op("gpsimd", lambda e: e.tensor_mul(mvb[:, :], pvo2[0:64, :], rcpb[0:64, :]), inc="pool")
        nrm_fin[(jp, qc)] = mv
        nrm_prev[0] = (jp, qc)
        p.wait("gpsimd", "pool", mv)
        for bh in range(4):
            b, hh = bh // 2, bh % 2
            h = jp * 2 + hh
            p.dma("gpsimd", OT[h * 64:(h + 1) * 64, b * SLOC + qc * 512: b * SLOC + qc * 512 + 512],
                  mvb[:, bh * 512:(bh + 1) * 512], "ot0")
    mk_evt = {}
    tt_hist = {}
    pv_last_of_jp = {}

    def emit_pv(jp, qc, kt, b, par):
        key = (jp, qc, kt, b)
        p.wait("tensor", "dve" if b == 0 else "pool", ev_tt[key])
        if kt == 0:
            prevq = ev_nrm.get((jp, qc - 1)) if qc > 0 else (ev_nrm.get((jp - 1, 1)) if jp > 0 else None)
            if prevq is not None:
                p.wait("tensor", "dve", prevq)
        for hh in range(2):
            bh = b * 2 + hh
            fn = lambda e, kt=kt, b=b, hh=hh, bh=bh, par=par: e.matmul(
                psB[0:65, ts(bh, 512)], vau[:, par, bh, kt * 65: kt * 65 + 65],
                pmb[:, kt % 2, b, hh * 512:(hh + 1) * 512],
                start=(kt == 0), stop=(kt == 63))
            if hh == 1:
                ev_pv[key] = p.op("tensor", fn, inc="pe")
            else:
                p.op("tensor", fn)

    def kv_issue(jpx):
        pr = jpx % 2
        c, jo = jpx // 2, jpx % 2
        if jpx >= 2:
            p.wait("gpsimd", "pe", pv_last_of_jp[jpx - 2])
        p.wait("gpsimd", "cc", cc_chunk[c])
        for b in range(2):
            for r in range(NCORE):
                p.dma("gpsimd", kst[:, pr, b, ts(r, 1024)],
                      K_AGc[c][r * 256 + jo * 128: r * 256 + jo * 128 + 128, ts(b, 1024)], "kv" + str(pr))
        for bh in range(4):
            b, hh = bh // 2, bh % 2
            hl = jo * 2 + hh
            for r in range(NCORE):
                p.dma("gpsimd", vau[:, pr, bh, :].rearrange("p (kt c) -> p kt c", c=65)[:, r * 8:(r + 1) * 8, :],
                      V_AGc[c][(r * 4 + hl) * T + b * SLOC: (r * 4 + hl) * T + b * SLOC + SLOC, :]
                      .rearrange("(kt p) c -> p kt c", p=128), "kv" + str(pr))
    if do_attn:
        kv_issue(0)
    for jp in range(KT if do_attn else 0):
        par = jp % 2
        p.dwait("tensor", "kv" + str(par))
        for qc in range(2):
            def emit_stage(skt):
                if skt % 2 == 0:
                    g = skt // 2
                    gpar = g % 2
                    gi = (jp * 2 + qc) * 32 + g
                    if gi >= 2 and (gi - 2, 0) in tt_hist:
                        p.wait("sync", "dve", tt_hist[(gi - 2, 0)])
                    if gi >= 2 and (gi - 2, 1) in tt_hist:
                        p.wait("sync", "pool", tt_hist[(gi - 2, 1)])
                    mk_evt[(jp, qc, g)] = p.dma(
                        "sync", mkb[:, gpar, :, :],
                        maskT[g * 256:(g + 1) * 256, qc * 512: qc * 512 + 512]
                        .rearrange("(t p) c -> p t c", p=128), "mk" + str(gpar))
                for b in range(2):
                    key = (jp, qc, skt, b)
                    ps_sc = psA[:, b * 1024:(b + 1) * 1024]
                    if skt >= 1:
                        p.wait("tensor", "act", ev_exp[(jp, qc, skt - 1, b)])
                    q0 = att_q[:, jp, b * SLOC + qc * 512: b * SLOC + qc * 512 + 512]
                    ksl = kst[:, par, b, skt * 128: skt * 128 + 128]
                    p.op("tensor", lambda e, ps_sc=ps_sc, ksl=ksl, q0=q0:
                         e.matmul(ps_sc[:, 0:512], ksl[0:64, :], q0[0:64, :],
                                  start=True, stop=True, tile_position=(0, 0)))
                    ev_mm[key] = p.op("tensor", lambda e, ps_sc=ps_sc, ksl=ksl, q0=q0:
                         e.matmul(ps_sc[:, 512:1024], ksl[64:128, :], q0[64:128, :],
                                  start=True, stop=True, tile_position=(64, 0)), inc="pe")
                    p.wait("scalar", "pe", ev_mm[key])
                    teng = "vector" if b == 0 else "gpsimd"
                    tsem = "dve" if b == 0 else "pool"
                    prev_tt2 = ev_tt.get((jp, qc, skt - 2, b))
                    if prev_tt2 is not None:
                        p.wait("scalar", tsem, prev_tt2)
                    ev_exp[key] = p.op("scalar", lambda e, ps_sc=ps_sc, skt=skt, b=b: e.activation(
                        out=peb[:, skt % 2, b, :], in_=ps_sc[:, :], func=AF.Exp), inc="act")
                    p.wait(teng, "act", ev_exp[key])
                    p.dwait(teng, "mk" + str((skt // 2) % 2))
                    prev_pv = ev_pv.get((jp, qc, skt - 2, b))
                    if prev_pv is not None:
                        p.wait(teng, "pe", prev_pv)
                    ev_tt[key] = p.op(teng, lambda e, skt=skt, b=b: e.tensor_tensor(
                        pmb[:, skt % 2, b, :].rearrange("p (two n) -> p two n", two=2),
                        peb[:, skt % 2, b, :].rearrange("p (two n) -> p two n", two=2),
                        mkb[:, (skt // 2) % 2, skt % 2, :].unsqueeze(1).broadcast_to([128, 2, 512]),
                        op=ALU.mult), inc=tsem)
                    tt_hist[((jp * 2 + qc) * 32 + skt // 2, b)] = ev_tt[key]
            emit_stage(0)
            for kt in range(64):
                if kt == 4 and pending_nrm:
                    pending_b.append(pending_nrm.pop(0))
                    emit_nrm_a(*pending_b[-1])
                if kt == 12 and pending_b:
                    pending_c.append(pending_b.pop(0))
                    emit_nrm_b(*pending_c[-1])
                if kt == 24 and pending_c:
                    emit_nrm_c(*pending_c.pop(0))
                if kt == 16 and qc == 0 and jp + 1 < KT:
                    kv_issue(jp + 1)
                if kt + 1 < 64:
                    emit_stage(kt + 1)
                emit_pv(jp, qc, kt, 0, par)
                emit_pv(jp, qc, kt, 1, par)
            p.wait("vector", "pe", ev_pv[(jp, qc, 63, 0)])
            p.wait("vector", "pe", ev_pv[(jp, qc, 63, 1)])
            prevf = nrm_fin.get((jp, qc - 1)) if qc > 0 else (nrm_fin.get((jp - 1, 1)) if jp > 0 else None)
            if prevf is not None:
                p.wait("vector", "pool", prevf)
            cpv = None
            for half2 in range(2):
                cpv = p.op("vector", lambda e, half2=half2: e.tensor_copy(
                    pvo2[:, half2 * 1024:(half2 + 1) * 1024],
                    psB[0:65, half2 * 1024: half2 * 1024 + 1024]), inc="dve")
            ev_nrm[(jp, qc)] = cpv
            pending_nrm.append((jp, qc))
        pv_last_of_jp[jp] = p.at("pe")
    while pending_c:
        emit_nrm_c(*pending_c.pop(0))
    while pending_b:
        k3 = pending_b.pop(0)
        emit_nrm_b(*k3)
        emit_nrm_c(*k3)
    while pending_nrm:
        k3 = pending_nrm.pop(0)
        emit_nrm_a(*k3)
        emit_nrm_b(*k3)
        emit_nrm_c(*k3)
    p.barrier()

    if do_wo:
        p.dwait("gpsimd", "ot0")
        p.dwait("gpsimd", "ot1")
        d = p.dma("gpsimd", arena_A[:, :, :], OT[:, :].rearrange("(kt p) t -> p kt t", p=128), "oto")
        p.dwait("tensor", "oto")
        p.dwait("scalar", "x12")
    res_use, res_st, res_cp = {}, {}, {}

    def wo_evac(dt, ps, par):
        a = p.op("scalar", lambda e, ps=ps, dt=dt, par=par: e.activation(
            out=evtmp[:, par, :], in_=ps[:, :], func=AF.Identity, bias=bias["bo"][:, dt:dt + 1]), inc="act")
        if dt >= 2:
            p.wait("scalar", "dve", res_use[dt - 2])
        dl = p.dma("scalar", xr_l[:, par, :], XR1[ts(dt, 128), :], "xr" + str(par))
        p.wait("vector", "act", a)
        p.dwait("vector", "xr" + str(par))
        if dt >= 1:
            p.dwait("vector", "x12")
            p.wait("vector", "act", res_cp[dt - 1])
        res_use[dt] = p.op("vector", lambda e, par=par, dt=dt: e.tensor_add(
            s1[:, :], evtmp[:, par, :], xr_l[:, par, :]), inc="dve")
        p.wait("sync", "dve", res_use[dt])
        p.wait("scalar", "dve", res_use[dt])
        res_st[dt] = p.dma("sync", XR2[ts(dt, 128), :], s1[:, :], "x12")
        res_cp[dt] = p.op("scalar", lambda e, dt=dt: e.activation(
            out=xbf[:, dt, :], in_=s1[:, :], func=AF.Copy), inc="act")
        return a
    if do_wo:
        emit_proj(Wo_t, KT, arena_A, wo_evac)
        p.dwait("vector", "x12")
        p.dwait("scalar", "x12")
        if "x2" in taps:
            tp = tap("x2", (D_, T))
            p.dwait("gpsimd", "x12")
            p.dma("gpsimd", tp[:, :], XR2[:, :], "tap")
            p.dwait("scalar", "tap")
            p.dwait("vector", "tap")
        p.barrier()
    if not _stop("lnx2"):
        emit_ln(arena_A)
        p.barrier()
    if not _stop("ffn2"):
        emit_ffn(1, tappfx="f2_")
        p.barrier()
    if not _stop("final"):
        emit_residual(XR2, outT, 0.5)
    for e in ENGS:
        p.dwait(e, "x12")
        p.dwait(e, "tap")
    p.barrier()

    with p.es:
        with nc.Block() as block:
            p.build(block)
    return nc, sorted(tap_t.keys())



def _tile_w(w):
    din, dout = w.shape
    ktn, dtn = din // 128, dout // 128
    return np.ascontiguousarray(
        w.reshape(ktn, 128, dtn, 128).transpose(2, 0, 1, 3).reshape(dtn * ktn * 128, 128))


def _col(v):
    return np.ascontiguousarray(v.reshape(-1, 128).T)


def _build_mask(c):
    import ml_dtypes
    i = np.arange(S_)
    xi = (i // GRID_) % GRID_
    yi = i % GRID_
    dxa = np.abs(xi[:, None] - xi[None, :])
    dya = np.abs(yi[:, None] - yi[None, :])
    dx = np.minimum(dxa, GRID_ - dxa).astype(np.float32)
    dy = np.minimum(dya, GRID_ - dya).astype(np.float32)
    dist = np.sqrt(dx * dx + dy * dy)
    m = np.exp(-np.float32(DECAY_) * dist).astype(np.float32)
    qi = np.arange(c * SLOC, (c + 1) * SLOC)
    sub = m[qi, :]
    tril = (np.arange(S_)[None, :] <= qi[:, None])
    mm = sub * tril + np.float32(1e-10)
    return np.ascontiguousarray(mm.T.astype(ml_dtypes.bfloat16))


def make_in_maps(inputs):
    import ml_dtypes
    f = np.float32
    x = np.asarray(inputs["L0_attn_hidden_states"], f)
    Wq = np.asarray(inputs["L0_attn_Wq"], f); bq = np.asarray(inputs["L0_attn_bq"], f)
    Wk = np.asarray(inputs["L0_attn_Wk"], f); bk = np.asarray(inputs["L0_attn_bk"], f)
    Wv = np.asarray(inputs["L0_attn_Wv"], f); bv = np.asarray(inputs["L0_attn_bv"], f)
    Wo = np.asarray(inputs["L0_attn_Wo"], f); bo = np.asarray(inputs["L0_attn_bo"], f)
    w_in = np.asarray(inputs["L0_ffn_in_proj_w"], f); b_in = np.asarray(inputs["L0_ffn_in_proj_b"], f)
    w_out = np.asarray(inputs["L0_ffn_out_proj_w"], f); b_out = np.asarray(inputs["L0_ffn_out_proj_b"], f)
    g1 = np.asarray(inputs["L0_ffn_ln1_g"], f); be1 = np.asarray(inputs["L0_ffn_ln1_b"], f)
    g2 = np.asarray(inputs["L0_ffn_ln2_g"], f); be2 = np.asarray(inputs["L0_ffn_ln2_b"], f)
    wg = np.asarray(inputs["L0_ffn_wg"], f)
    w1 = np.asarray(inputs["L0_ffn_w1"], f); b1 = np.asarray(inputs["L0_ffn_b1"], f)
    w2 = np.asarray(inputs["L0_ffn_w2"], f); b2 = np.asarray(inputs["L0_ffn_b2"], f)

    in_maps = []
    for c in range(NCORE):
        xc = x[:, c * SLOC:(c + 1) * SLOC, :].reshape(T, D_).T
        sel = np.zeros((16, 2), np.float32)
        for r in range(NCORE):
            if r < c:
                sel[r * 2 + 0, 0] = 1.0
                sel[r * 2 + 1, 1] = 1.0
            sel[r * 2 + 0, 1] = 1.0
        von = np.zeros((NH * T, 65), ml_dtypes.bfloat16)
        von[:, 64] = 1.0
        im = {
            "xT": np.ascontiguousarray(xc),
            "Wq_t": _tile_w(Wq), "Wk_t": _tile_w(Wk), "Wo_t": _tile_w(Wo),
            "Wv_nat": np.ascontiguousarray(Wv),
            "bq_c": _col(bq * 0.125), "bk_c": _col(bk), "bo_c": _col(bo),
            "bv_h": np.ascontiguousarray(bv.reshape(NH, 64).T),
            "Win_t": _tile_w(w_in.T), "bin_c": _col(b_in),
            "Wout_t": _tile_w(w_out.T), "bout_c": _col(b_out),
            "g1_c": _col(g1), "be1_c": _col(be1), "g2_c": _col(g2), "be2_c": _col(be2),
            "wg_in": np.ascontiguousarray(wg),
            "w1T_t": np.ascontiguousarray(
                w1[c].T.reshape(32, 128, KT, 128).transpose(2, 0, 1, 3).reshape(KT * 32 * 128, 128)),
            "w2_nat": np.ascontiguousarray(w2[c]),
            "b1_c": _col(b1[c]),
            "b2_row": np.ascontiguousarray(b2[c].reshape(1, D_)),
            "maskT": _build_mask(c),
            "gate_r": np.broadcast_to(8.0 - np.arange(E_, dtype=np.float32), (128, E_)).copy(),
            "moe_sel": sel,
            "vones": von,
        }
        in_maps.append(im)
    return in_maps


def kernel(**inputs):
    nc, _ = build_nc()
    in_maps = make_in_maps(inputs)
    res = run_bass_kernel_spmd(nc, in_maps, core_ids=list(range(NCORE)))
    out = np.empty((B_, S_, D_), np.float32)
    for c in range(NCORE):
        o = res.results[c]["outT"]
        out[0, c * SLOC:(c + 1) * SLOC, :] = o[:, 0:SLOC].T
        out[1, c * SLOC:(c + 1) * SLOC, :] = o[:, SLOC:T].T
    return out


# version 13
# speedup vs baseline: 1.1620x; 1.1620x over previous
import contextlib
import numpy as np

import concourse.bass as bass
import concourse.mybir as mybir
from concourse.bass_utils import run_bass_kernel_spmd

F32 = mybir.dt.float32
BF16 = mybir.dt.bfloat16
AF = mybir.ActivationFunctionType
ALU = mybir.AluOpType
AX = mybir.AxisListType

B_, S_, D_ = 2, 8192, 1024
NCORE = 8
SLOC = S_ // NCORE
T = 2 * SLOC
KT = D_ // 128
NH, HD = 16, 64
E_, FF_ = 8, 4096
CAP = (B_ * S_) // E_
EPS = 1e-5
GRID_, DECAY_ = 12, 0.3
WROW = KT * KT * 128 + KT

ENGS = ("sync", "scalar", "vector", "tensor", "gpsimd")


class Prog:
    def __init__(self, nc):
        self.nc = nc
        self.q = {e: [] for e in ENGS}
        self.sems = {}
        self.cnt = {}
        self.es = contextlib.ExitStack()

    def sem(self, name):
        if name not in self.sems:
            self.sems[name] = self.es.enter_context(self.nc.semaphore(name))
            self.cnt[name] = 0
        return self.sems[name]

    def op(self, eng, fn, inc=None, inc_n=1):
        if inc is not None:
            s = self.sem(inc)
            self.cnt[inc] += inc_n
            self.q[eng].append(lambda e, fn=fn, s=s, n=inc_n: fn(e).then_inc(s, n))
            return self.cnt[inc]
        self.q[eng].append(lambda e, fn=fn: fn(e))
        return None

    def dma(self, eng, out, in_, stream, **kw):
        s = self.sem("D_" + stream)
        self.cnt["D_" + stream] += 16
        self.q[eng].append(lambda e, o=out, i=in_, s=s, kw=kw: e.dma_start(out=o, in_=i, **kw).then_inc(s, 16))
        return self.cnt["D_" + stream]

    def dwait(self, eng, stream):
        self.wait(eng, "D_" + stream)

    def wait(self, eng, semname, val=None):
        s = self.sem(semname)
        v = self.cnt[semname] if val is None else val
        if v > 0:
            self.q[eng].append(lambda e, s=s, v=v: e.wait_ge(s, v))

    def at(self, semname):
        self.sem(semname)
        return self.cnt[semname]

    def barrier(self):
        s = self.sem("bar")
        self.cnt["bar"] += len(ENGS)
        tgt = self.cnt["bar"]
        for e in ENGS:
            self.q[e].append(lambda eng, s=s: (eng.drain(), eng.sem_inc(s, 1)))
            self.q[e].append(lambda eng, s=s, t=tgt: eng.wait_ge(s, t))

    def build(self, block):
        for ename in ENGS:
            ops = list(self.q[ename])
            section = getattr(block, ename)

            def body(eng, ops=ops):
                for f in ops:
                    f(eng)
            section(body)


def ts(i, n=128):
    return slice(i * n, (i + 1) * n)


def build_nc(taps=(), upto=None):
    nc = bass.Bass("TRN2", target_bir_lowering=False, debug=False)
    p = Prog(nc)
    taps = set(taps)
    tap_t = {}

    def din(name, shape, dtype=F32):
        return nc.dram_tensor(name, list(shape), dtype, kind="ExternalInput").ap()

    def dout(name, shape, dtype=F32):
        return nc.dram_tensor(name, list(shape), dtype, kind="ExternalOutput").ap()

    def dint(name, shape, dtype=F32, shared=False):
        kw = {"addr_space": "Shared"} if shared else {}
        return nc.dram_tensor(name, list(shape), dtype, **kw).ap()

    def tap(name, shape, dtype=F32):
        if name in taps and name not in tap_t:
            tap_t[name] = dout("tap_" + name, shape, dtype)
        return tap_t.get(name)

    xT = din("xT", (D_, T))
    Wq_t = din("Wq_t", (KT * KT * 128, 128))
    Wk_t = din("Wk_t", (KT * KT * 128, 128))
    Wv_nat = din("Wv_nat", (D_, D_))
    Wo_t = din("Wo_t", (KT * KT * 128, 128))
    bq_c = din("bq_c", (128, KT))
    bk_c = din("bk_c", (128, KT))
    bv_h = din("bv_h", (64, NH))
    bo_c = din("bo_c", (128, KT))
    Win_t = din("Win_t", (24 * KT * 128, 128))
    bin_c = din("bin_c", (128, 24))
    Wout_t = din("Wout_t", (KT * KT * 128, 128))
    bout_c = din("bout_c", (128, KT))
    g1_c = din("g1_c", (128, KT)); be1_c = din("be1_c", (128, KT))
    g2_c = din("g2_c", (128, KT)); be2_c = din("be2_c", (128, KT))
    wg_in = din("wg_in", (D_, E_))
    w1T_t = din("w1T_t", (KT * 32 * 128, 128))
    w2_nat = din("w2_nat", (FF_, D_))
    b1_c = din("b1_c", (128, 32))
    b2_row = din("b2_row", (1, D_))
    maskT = din("maskT", (S_, SLOC), BF16)
    gate_r = din("gate_r", (128, E_))
    moe_sel = din("moe_sel", (16, 2))
    vones = din("vones", (NH * T, 65), BF16)

    outT = dout("outT", (D_, T))

    XR1 = dint("XR1", (D_, T))
    XR2 = dint("XR2", (D_, T))
    ROWS = dint("ROWS", (8, 4096))
    GM = [dint("GM_A", (E_, T)), dint("GM_B", (E_, T))]
    CNTl = [dint("CNTA_l", (2, E_)), dint("CNTB_l", (2, E_))]
    CNTg = [dint("CNTA_g", (16, E_), shared=True), dint("CNTB_g", (16, E_), shared=True)]
    K_loc = dint("K_loc", (D_, T), BF16)
    V_loc = dint("V_loc", (NH * T, 65), BF16)
    K_AGc = [dint("K_AGc%d" % c, (NCORE * 256, T), BF16, shared=True) for c in range(4)]
    V_AGc = [dint("V_AGc%d" % c, (NCORE * 4 * T, 65), BF16, shared=True) for c in range(4)]
    W_loc = dint("W_loc", (WROW, 128), BF16)
    W_AG = dint("W_AG", (NCORE * WROW, 128), BF16, shared=True)
    OT = dint("OT", (D_, T), BF16)

    W_BIG = 51200
    BIG = p.es.enter_context(nc.sbuf_tensor("BIG", [128, W_BIG], F32))
    BIGb = BIG.bitcast(BF16)

    def vf(off, *shape, parts=128):
        n = 1
        for s in shape:
            n *= s
        ap = BIG[0:parts, off:off + n]
        if len(shape) > 1:
            pat = "p (" + " ".join(f"d{i}" for i in range(len(shape))) + ") -> p " + \
                  " ".join(f"d{i}" for i in range(len(shape)))
            ap = ap.rearrange(pat, **{f"d{i}": s for i, s in enumerate(shape)})
        return ap

    def vb(off, *shape, parts=128):
        n = 1
        for s in shape:
            n *= s
        ap = BIGb[0:parts, 2 * off:2 * off + n]
        if len(shape) > 1:
            pat = "p (" + " ".join(f"d{i}" for i in range(len(shape))) + ") -> p " + \
                  " ".join(f"d{i}" for i in range(len(shape)))
            ap = ap.rearrange(pat, **{f"d{i}": s for i, s in enumerate(shape)})
        return ap

    arena_A = vb(0, KT, T)
    xbf = vb(8192, KT, T)
    evtmp = vb(16384, 2, T)
    s1 = vf(18432, T)
    rows_s = vf(20480, 6144, parts=1)
    w2_s = vb(26624, 32, D_)
    w1b = vb(43008, 2, 32, 128)
    wcol = vb(47104, 2, D_)
    crow_s = vf(48128, D_, parts=1)
    crow_b = vb(49152, D_, parts=1)
    b2r_s = vf(49664, D_, parts=1)
    arena_Q = vb(26624, KT, T)
    arena_K = vb(34816, KT, SLOC)
    arena_V1 = vb(38912, KT, SLOC)
    arena_VD = vb(43008, KT, SLOC)
    wbuf = vb(47104, 2, KT, 128)
    prod_s = vb(48128, 2, SLOC)
    pbc2 = vb(49152, 2, SLOC)
    sm4 = vf(34816, T, parts=4)
    sm4b = vf(36864, T, parts=4)
    sqb = vb(38912, 2, KT, 512)
    abc_bc = vf(43008, 2, T)
    lg_row = vf(26624, T, parts=8)
    gmr = vf(28672, T, parts=8)
    glt = vf(30720, 16, E_)
    g_oh = vf(30848, 16, E_)
    g_w = vf(30976, 16, E_)
    g_u = vf(31104, 16, E_)
    fst_b = vf(31232, 16, E_)
    gexp = vf(31360, 16, E_)
    cums = vf(31488, 16, E_)
    g_mx = vf(31616, 16)
    g_sm = vf(31648, 16)
    g_gv = vf(31680, 16)
    g_keep = vf(31712, 16)
    g_gm = vf(31744, 16)
    g_loc = vf(31776, 16)
    offb = vf(31808, 2, E_)
    r_row = vf(31840, E_, parts=1)
    cnt_row = vf(31872, 16, parts=1)
    offs_v = vf(31936, E_, parts=2)
    cg_s = vf(31968, E_, parts=16)
    wmoe_a = vb(16384, E_, KT, 128)
    wmoe_b = vb(20480, E_, KT, 128)
    cmoe = vb(24576, E_ * KT * 128, parts=1)
    gm_ch = vb(28672, E_, 512)
    zbuf = vb(30720, E_, KT, 512)
    att_q = vb(33024, KT, T)
    wv_s = vb(43008, KT, D_)
    kst = vb(0, 2, 2, S_)
    vau = vb(16384, 2, 4, 64 * 65)
    peb = vb(41216, 2, 2, 1024)
    pmb = vb(43264, 2, 2, 1024)
    mkb = vb(45312, 2, 2, 512)
    rcp65 = vf(46336, 2048, parts=65)
    obf = vb(50432, 512, parts=64)
    rcpb = vb(48384, 2048, parts=128)
    pvo2 = vb(49408, 2048, parts=65)
    dsm = vf(50688, 16, parts=128)
    dsm2 = vf(50704, 16, parts=128)
    obf2 = vb(50720, 512, parts=64)
    mvb = vb(46336, 2048, parts=64)
    bvbc = vf(50176, 1024)

    xr_l = vf(41216, 2, T)

    def sbuf(name, shape, dtype=F32):
        return p.es.enter_context(nc.sbuf_tensor(name, list(shape), dtype))

    ones_col = sbuf("ones_col", (128, 1), BF16)
    ones_col_f = sbuf("ones_col_f", (128, 1), F32)
    ones_row_f = sbuf("ones_row_f", (1, 128), F32)
    onesr64 = sbuf("onesr64", (65, 64), F32)
    eps_t = sbuf("eps_t", (1, 1), F32)
    triu_f = sbuf("triu_f", (128, 128), F32)
    id_f = sbuf("id_f", (128, 128), F32)
    gater = sbuf("gater", (128, E_), F32)
    selmat = sbuf("selmat", (16, 2), F32)
    bias = {}
    for nm, w in [("bq", KT), ("bk", KT), ("bo", KT), ("bin", 24), ("bout", KT),
                  ("g1", KT), ("be1", KT), ("g2", KT), ("be2", KT)]:
        bias[nm] = sbuf("c_" + nm, (128, w), F32)
    bvh_s = sbuf("bvh_s", (64, NH), F32)
    b1_s = sbuf("b1_s", (128, 32), BF16)
    wg_s = sbuf("wg_s", (128, KT, E_), BF16)
    hsel_s = sbuf("hsel_s", (128, 4 * KT), BF16)

    psA = p.es.enter_context(nc.psum_tensor("psA", [128, 2048], F32))
    psB = p.es.enter_context(nc.psum_tensor("psB", [128, 2048], F32))

    p.op("gpsimd", lambda e: e.memset(ones_col[:], 1.0))
    p.op("gpsimd", lambda e: e.memset(ones_col_f[:], 1.0))
    p.op("gpsimd", lambda e: e.memset(ones_row_f[:], 1.0))
    p.op("gpsimd", lambda e: e.memset(onesr64[64:65, :], 1.0))
    p.op("gpsimd", lambda e: e.memset(eps_t[:], EPS))
    g = p.op("gpsimd", lambda e: e.memset(triu_f[:], 1.0), inc="pool")
    p.wait("gpsimd", "pool", g)
    p.op("gpsimd", lambda e: e.affine_select(out=triu_f[:], in_=triu_f[:], pattern=[[1, 128]],
                                             compare_op=ALU.is_ge, fill=0.0, base=0, channel_multiplier=-1))
    g = p.op("gpsimd", lambda e: e.memset(id_f[:], 1.0), inc="pool")
    p.wait("gpsimd", "pool", g)
    p.op("gpsimd", lambda e: e.affine_select(out=id_f[:], in_=id_f[:], pattern=[[-1, 128]],
                                             compare_op=ALU.is_equal, fill=0.0, base=0, channel_multiplier=1))
    g = p.op("gpsimd", lambda e: e.memset(hsel_s[:], 0.0), inc="pool")
    p.wait("gpsimd", "pool", g)
    for kt in range(KT):
        col = kt * 4 + kt // 2
        p.op("gpsimd", lambda e, col=col: e.memset(hsel_s[:, col:col + 1], 1.0))
    p.dma("gpsimd", gater[:, :], gate_r[:, :], "ini")
    p.dma("gpsimd", selmat[:, :], moe_sel[:, :], "ini")
    for nm, src in [("bq", bq_c), ("bk", bk_c), ("bo", bo_c), ("bin", bin_c), ("bout", bout_c),
                    ("g1", g1_c), ("be1", be1_c), ("g2", g2_c), ("be2", be2_c)]:
        p.dma("gpsimd", bias[nm][:, :], src[:, :], "ini")
    p.dma("gpsimd", bvh_s[:, :], bv_h[:, :], "ini")
    p.dma("gpsimd", b1_s[:, :], b1_c[:, :], "ini")
    p.dma("gpsimd", b2r_s[0:1, :], b2_row[:, :], "ini")
    p.dma("gpsimd", wg_s[:, :, :], wg_in.rearrange("(kt p) e -> p kt e", p=128), "ini",
          allow_slow_non_contiguous=True)
    p.dma("sync", V_loc[:, :], vones[:, :], "von")
    for _e in ENGS:
        p.dwait(_e, "ini")
        p.dwait(_e, "von")
    p.barrier()

    def _stop(name):
        return upto is not None and _PHASES.index(upto) < _PHASES.index(name)
    _PHASES = ["init", "collapse", "ln0", "ffn1", "res1", "ain", "qkv", "attn", "wo", "lnx2", "ffn2", "final"]
    do_collapse = not _stop("collapse")
    if do_collapse:
      p.dma("gpsimd", w2_s[:, :, :], w2_nat.rearrange("(kf p) n -> p kf n", p=128), "cw2")
      p.dwait("tensor", "cw2")
      wcol_evt = {}
      cmm_evt = {}
      for md in range(KT):
          par = md % 2
          if md >= 2:
              p.wait("gpsimd", "pe", cmm_evt[md - 2])
          d = p.dma("gpsimd", w1b[:, par, :, :],
                    w1T_t[md * 32 * 128:(md + 1) * 32 * 128, :].rearrange("(kf p) m -> p kf m", p=128), "cw1" + str(par))
          if md >= 2:
              p.wait("tensor", "dve", wcol_evt[md - 2])
          p.dwait("tensor", "cw1" + str(par))
          ps = psA if par == 0 else psB
          pe_done = None
          for nch in range(2):
              for kf in range(32):
                  pe_done = p.op("tensor", lambda e, ps=ps, kf=kf, nch=nch, par=par:
                       e.matmul(ps[:, ts(nch, 512)], w1b[:, par, kf, :], w2_s[:, kf, ts(nch, 512)],
                                start=(kf == 0), stop=(kf == 31)),
                       inc="pe" if (nch == 1 and kf == 31) else None)
          cmm_evt[md] = pe_done
          p.wait("vector", "pe", pe_done)
          if md >= 2:
              p.dwait("vector", "wst")
          wcol_evt[md] = p.op("vector", lambda e, ps=ps, par=par:
                              e.tensor_copy(wcol[:, par, :], ps[:, 0:1024]), inc="dve")
          p.wait("scalar", "dve", wcol_evt[md])
          p.dma("scalar",
                W_loc[0:KT * KT * 128, :].rearrange("(dt md p) m -> p md dt m", dt=KT, md=KT)[:, md, :, :],
                wcol[:, par, :].rearrange("p (dt m) -> p dt m", dt=KT), "wst")
      p.wait("tensor", "dve")
      pe_done = None
      for nch in range(2):
          for kf in range(32):
              pe_done = p.op("tensor", lambda e, kf=kf, nch=nch:
                   e.matmul(psA[0:1, ts(nch, 512)], b1_s[:, kf:kf + 1], w2_s[:, kf, ts(nch, 512)],
                            start=(kf == 0), stop=(kf == 31)),
                   inc="pe" if (nch == 1 and kf == 31) else None)
      p.wait("vector", "pe", pe_done)
      v = p.op("vector", lambda e: e.tensor_add(crow_s[0:1, :], psA[0:1, 0:1024], b2r_s[0:1, :]), inc="dve")
      p.wait("vector", "dve", v)
      v = p.op("vector", lambda e: e.tensor_copy(crow_b[0:1, :], crow_s[0:1, :]), inc="dve")
      p.wait("scalar", "dve", v)
      p.dma("scalar", W_loc[KT * KT * 128: WROW, :].rearrange("dt m -> (dt m)").unsqueeze(0),
            crow_b[0:1, :], "wst")
      p.dwait("gpsimd", "wst")
      p.op("gpsimd", lambda e: e.collective_compute(
          "AllGather", ALU.bypass, replica_groups=[list(range(NCORE))],
          ins=[W_loc], outs=[W_AG]), inc="cc")
      CC_W = p.at("cc")
    if not do_collapse:
        CC_W = 0

    def emit_ln(dst, gain=None, beta=None, tapnm=None):
        p.wait("vector", "dve")
        p.wait("vector", "act")
        p.wait("tensor", "act")
        p.wait("tensor", "dve")
        sq_evt, ev_evt = {}, {}
        for ch in range(4):
            par = ch % 2
            if ch >= 2:
                p.wait("vector", "pe", sq_evt[ch - 2])
            v = p.op("vector", lambda e, ch=ch, par=par:
                     e.tensor_mul(sqb[:, par, :, :], xbf[:, :, ts(ch, 512)], xbf[:, :, ts(ch, 512)]), inc="dve")
            if ch >= 2:
                p.wait("tensor", "dve", ev_evt[ch - 2])
            ps = psA if par == 0 else psB
            for kt in range(KT):
                p.op("tensor", lambda e, ps=ps, kt=kt, ch=ch:
                     e.matmul(ps[0:1, 0:512], ones_col[:, :], xbf[:, kt, ts(ch, 512)],
                              start=(kt == 0), stop=(kt == KT - 1)))
            p.wait("tensor", "dve", v)
            for kt in range(KT):
                sq_evt[ch] = p.op("tensor", lambda e, ps=ps, kt=kt, par=par:
                     e.matmul(ps[0:1, 512:1024], ones_col[:, :], sqb[:, par, kt, :],
                              start=(kt == 0), stop=(kt == KT - 1)),
                     inc="pe" if kt == KT - 1 else None)
            p.wait("vector", "pe", sq_evt[ch])
            ev_evt[ch] = p.op("vector", lambda e, ps=ps, ch=ch: e.tensor_copy(
                rows_s[0:1, 0:4096].rearrange("a (h c f) -> a h c f", h=2, c=4)[:, :, ch, :],
                ps[0:1, 0:1024].rearrange("a (h f) -> a h f", h=2)), inc="dve")
        p.wait("vector", "dve")
        v = p.op("vector", lambda e: e.tensor_scalar(rows_s[0:1, 0:4096], rows_s[0:1, 0:4096],
                                                     1.0 / D_, None, op0=ALU.mult), inc="dve")
        p.wait("vector", "dve", v)
        v = p.op("vector", lambda e: e.tensor_mul(rows_s[0:1, 4096:6144], rows_s[0:1, 0:2048],
                                                  rows_s[0:1, 0:2048]), inc="dve")
        p.wait("vector", "dve", v)
        v = p.op("vector", lambda e: e.tensor_sub(rows_s[0:1, 2048:4096], rows_s[0:1, 2048:4096],
                                                  rows_s[0:1, 4096:6144]), inc="dve")
        p.wait("scalar", "dve", v)
        a = p.op("scalar", lambda e: e.activation(out=rows_s[0:1, 4096:6144], in_=rows_s[0:1, 2048:4096],
                                                  func=AF.Ln, bias=eps_t[0:1, 0:1]), inc="act")
        p.wait("scalar", "act", a)
        a = p.op("scalar", lambda e: e.activation(out=rows_s[0:1, 2048:4096], in_=rows_s[0:1, 4096:6144],
                                                  func=AF.Exp, scale=-0.5), inc="act")
        p.wait("vector", "act", a)
        v = p.op("vector", lambda e: e.scalar_tensor_tensor(out=rows_s[0:1, 4096:6144],
                                                            in0=rows_s[0:1, 0:2048], scalar=-1.0,
                                                            in1=rows_s[0:1, 2048:4096],
                                                            op0=ALU.mult, op1=ALU.mult), inc="dve")
        p.wait("gpsimd", "dve", v)
        d = p.dma("gpsimd", ROWS[0:1, :], rows_s[0:1, 2048:6144], "lnr")
        p.dwait("gpsimd", "lnr")
        d = p.dma("gpsimd", abc_bc[:, :, :],
                  ROWS[0:1, :].rearrange("a (b t) -> (a b) t", b=2).partition_broadcast(128), "lna")
        p.dwait("vector", "lna")
        for kt in range(KT):
            v = p.op("vector", lambda e, kt=kt: e.tensor_mul(s1[:, :], xbf[:, kt, :], abc_bc[:, 0, :]), inc="dve")
            p.wait("vector", "dve", v)
            v = p.op("vector", lambda e, kt=kt, dst=dst: e.tensor_add(dst[:, kt, :], s1[:, :], abc_bc[:, 1, :]), inc="dve")
            p.wait("vector", "dve", v)
            if gain is not None:
                v = p.op("vector", lambda e, kt=kt, dst=dst, gain=gain, beta=beta: e.tensor_scalar(
                    dst[:, kt, :], dst[:, kt, :], gain[:, kt:kt + 1], beta[:, kt:kt + 1],
                    op0=ALU.mult, op1=ALU.add), inc="dve")
                p.wait("vector", "dve", v)
        if tapnm is not None:
            tp = tap(tapnm, (D_, T), BF16)
            if tp is not None:
                p.wait("gpsimd", "dve")
                p.dma("gpsimd", tp[:, :].rearrange("(kt p) t -> p kt t", p=128), dst[:, :, :], "tap")
                p.dwait("vector", "tap")
                p.dwait("scalar", "tap")
                p.dwait("tensor", "tap")

    def emit_proj(wtile, nd, rhs_arena, evac_fn):
        evt = {}
        pevt = {}
        p.wait("gpsimd", "pe")
        p.wait("tensor", "act")
        p.wait("tensor", "dve")
        for dt in range(nd):
            par = dt % 2
            if dt >= 2:
                p.wait("gpsimd", "pe", pevt[dt - 2])
            d = p.dma("gpsimd", wbuf[:, par, :, :],
                      wtile[dt * KT * 128:(dt + 1) * KT * 128, :].rearrange("(kt p) m -> p kt m", p=128), "wb" + str(par))
            if dt >= 2:
                p.wait("tensor", "act", evt[dt - 2])
            p.dwait("tensor", "wb" + str(par))
            ps = psA if par == 0 else psB
            mmdone = None
            for nch in range(4):
                for kt in range(KT):
                    mmdone = p.op("tensor", lambda e, ps=ps, kt=kt, nch=nch, par=par:
                         e.matmul(ps[:, ts(nch, 512)], wbuf[:, par, kt, :], rhs_arena[:, kt, ts(nch, 512)],
                                  start=(kt == 0), stop=(kt == KT - 1)),
                         inc="pe" if (nch == 3 and kt == KT - 1) else None)
            pevt[dt] = mmdone
            p.wait("scalar", "pe", mmdone)
            evt[dt] = evac_fn(dt, ps, par)
        return evt

    def emit_ffn(idx, tappfx=""):
        def qkv_evac(dt, ps, par):
            if dt < 8:
                return p.op("scalar", lambda e, ps=ps, dt=dt: e.activation(
                    out=arena_Q[:, dt, :], in_=ps[:, :], func=AF.Identity, bias=bias["bin"][:, dt:dt + 1]), inc="act")
            a = p.op("scalar", lambda e, ps=ps, par=par: e.activation(
                out=evtmp[:, par, :], in_=ps[:, :], func=AF.Copy), inc="act")
            p.wait("vector", "act", a)
            if dt < 16:
                kt8 = dt - 8
                v = p.op("vector", lambda e, par=par, kt8=kt8: e.tensor_sub(
                    arena_K[:, kt8, :], evtmp[:, par, 0:SLOC], evtmp[:, par, SLOC:T]), inc="dve")
            else:
                kt8 = dt - 16
                v = p.op("vector", lambda e, par=par, kt8=kt8: e.tensor_sub(
                    arena_VD[:, kt8, :], evtmp[:, par, 0:SLOC], evtmp[:, par, SLOC:T]), inc="dve")
                p.wait("vector", "dve", v)
                v = p.op("vector", lambda e, par=par, kt8=kt8, dt=dt: e.tensor_scalar(
                    arena_V1[:, kt8, :], evtmp[:, par, SLOC:T], bias["bin"][:, dt:dt + 1], None,
                    op0=ALU.add), inc="dve")
            p.wait("scalar", "dve", v)
            return a
        emit_proj(Win_t, 24, arena_A, qkv_evac)

        p.wait("vector", "act")
        p.wait("vector", "dve")
        p.wait("tensor", "act")
        p.wait("tensor", "dve")
        prod_evt = {}
        for l in range(2):
            ps = psA if l == 0 else psB
            for kt in range(KT):
                par = kt % 2
                g8 = l * KT + kt
                if g8 >= 2:
                    p.wait("vector", "pe", prod_evt[g8 - 2])
                v = p.op("vector", lambda e, l=l, kt=kt, par=par: e.tensor_mul(
                    prod_s[:, par, :], arena_Q[:, kt, ts(l, SLOC)], arena_K[:, kt, :]), inc="dve")
                p.wait("tensor", "dve", v)
                for nch in range(2):
                    prod_evt[g8] = p.op("tensor", lambda e, ps=ps, kt=kt, nch=nch, par=par:
                         e.matmul(ps[0:4, ts(nch, 512)], hsel_s[:, kt * 4:(kt + 1) * 4],
                                  prod_s[:, par, ts(nch, 512)], start=(kt == 0), stop=(kt == KT - 1)),
                         inc="pe" if nch == 1 else None)
        p.wait("scalar", "pe")
        for l in range(2):
            ps = psA if l == 0 else psB
            p.op("scalar", lambda e, ps=ps, l=l: e.activation(
                out=sm4[0:4, ts(l, SLOC)], in_=ps[0:4, 0:SLOC], func=AF.Exp, scale=-1.0 / 16.0), inc="act")
        p.wait("vector", "act")
        v = p.op("vector", lambda e: e.tensor_scalar(sm4b[0:4, :], sm4[0:4, :], 1.0, None, op0=ALU.add), inc="dve")
        p.wait("vector", "dve", v)
        v = p.op("vector", lambda e: e.reciprocal(sm4[0:4, :], sm4b[0:4, :]), inc="dve")
        p.wait("gpsimd", "dve", v)
        d = p.dma("gpsimd", ROWS[0:4, 0:T], sm4[0:4, :], "lnr")
        p.dwait("gpsimd", "lnr")
        for l in range(2):
            pdma = {0: p.dma("gpsimd", pbc2[:, 0, :], ROWS[0:1, ts(l, SLOC)].partition_broadcast(128), "pb0")}
            for kt in range(KT):
                if kt + 1 < KT:
                    h2 = (kt + 1) // 2
                    pdma[kt + 1] = p.dma("gpsimd", pbc2[:, (kt + 1) % 2, :],
                                         ROWS[h2:h2 + 1, ts(l, SLOC)].partition_broadcast(128), "pb" + str((kt + 1) % 2))
                p.dwait("vector", "pb" + str(kt % 2))
                v = p.op("vector", lambda e, kt=kt: e.tensor_mul(
                    prod_s[:, 0, :], pbc2[:, kt % 2, :], arena_VD[:, kt, :]), inc="dve")
                p.wait("vector", "dve", v)
                v = p.op("vector", lambda e, kt=kt, l=l: e.tensor_add(
                    arena_Q[:, kt, ts(l, SLOC)], prod_s[:, 0, :], arena_V1[:, kt, :]), inc="dve")
                p.wait("vector", "dve", v)
                p.wait("gpsimd", "dve", v)
        def op_evac(dt, ps, par):
            a = p.op("scalar", lambda e, ps=ps, dt=dt, par=par: e.activation(
                out=evtmp[:, par, :], in_=ps[:, :], func=AF.Identity, bias=bias["bout"][:, dt:dt + 1]), inc="act")
            p.wait("vector", "act", a)
            v = p.op("vector", lambda e, dt=dt, par=par: e.tensor_add(
                xbf[:, dt, :], evtmp[:, par, :], arena_A[:, dt, :]), inc="dve")
            p.wait("scalar", "dve", v)
            return a
        p.wait("tensor", "dve")
        emit_proj(Wout_t, KT, arena_Q, op_evac)
        if tappfx + "y1pre" in taps:
            tp = tap(tappfx + "y1pre", (D_, T), BF16)
            p.wait("gpsimd", "dve")
            p.dma("gpsimd", tp[:, :].rearrange("(kt p) t -> p kt t", p=128), xbf[:, :, :], "tap")
            p.dwait("vector", "tap")
            p.dwait("scalar", "tap")
        emit_ln(arena_A, gain=bias["g1"], beta=bias["be1"],
                tapnm=(tappfx + "y1") if (tappfx + "y1") in taps else None)

        p.wait("tensor", "dve")
        mmdone = None
        for nch in range(4):
            for kt in range(KT):
                mmdone = p.op("tensor", lambda e, kt=kt, nch=nch:
                     e.matmul(psA[0:8, ts(nch, 512)], wg_s[:, kt, :], arena_A[:, kt, ts(nch, 512)],
                              start=(kt == 0), stop=(kt == KT - 1)),
                     inc="pe" if (nch == 3 and kt == KT - 1) else None)
        p.wait("vector", "pe", mmdone)
        v = p.op("vector", lambda e: e.tensor_copy(lg_row[0:8, :], psA[0:8, 0:T]), inc="dve")
        p.wait("tensor", "dve", v)
        for tt in range(16):
            mmdone = p.op("tensor", lambda e, tt=tt: e.transpose(psB[:, tt * 8:(tt + 1) * 8],
                                                        lg_row[0:8, ts(tt, 128)], id_f[0:8, 0:8]),
                          inc="pe" if tt == 15 else None)
        p.wait("vector", "pe", mmdone)
        v = p.op("vector", lambda e: e.tensor_copy(
            glt[:, :, :], psB[:, 0:128].rearrange("p (tt e) -> p tt e", e=8)), inc="dve")
        p.wait("vector", "dve", v)
        v = p.op("vector", lambda e: e.reduce_max(g_mx[:, :], glt[:, :, :], axis=AX.X), inc="dve")
        p.wait("vector", "dve", v)
        v = p.op("vector", lambda e: e.tensor_tensor(
            g_oh[:, :, :], glt[:, :, :], g_mx[:, :].unsqueeze(2).broadcast_to([128, 16, 8]),
            op=ALU.is_ge), inc="dve")
        p.wait("vector", "dve", v)
        v = p.op("vector", lambda e: e.tensor_tensor(
            g_w[:, :, :], g_oh[:, :, :], gater[:, :].unsqueeze(1).broadcast_to([128, 16, 8]),
            op=ALU.mult), inc="dve")
        p.wait("vector", "dve", v)
        v = p.op("vector", lambda e: e.reduce_max(g_sm[:, :], g_w[:, :, :], axis=AX.X), inc="dve")
        p.wait("vector", "dve", v)
        v = p.op("vector", lambda e: e.tensor_tensor(
            fst_b[:, :, :], g_w[:, :, :], g_sm[:, :].unsqueeze(2).broadcast_to([128, 16, 8]),
            op=ALU.is_ge), inc="dve")
        p.wait("vector", "dve", v)
        v = p.op("vector", lambda e: e.tensor_tensor(
            gexp[:, :, :], glt[:, :, :], g_mx[:, :].unsqueeze(2).broadcast_to([128, 16, 8]),
            op=ALU.subtract), inc="dve")
        p.wait("scalar", "dve", v)
        a = p.op("scalar", lambda e: e.activation(out=gexp[:, :, :], in_=gexp[:, :, :], func=AF.Exp), inc="act")
        p.wait("vector", "act", a)
        v = p.op("vector", lambda e: e.reduce_sum(g_sm[:, :], gexp[:, :, :], axis=AX.X), inc="dve")
        p.wait("vector", "dve", v)
        v = p.op("vector", lambda e: e.reciprocal(g_gv[:, :], g_sm[:, :]), inc="dve")
        p.wait("tensor", "dve", v)
        cum_evt = {}
        for seg in range(2):
            rz = p.op("vector", lambda e: e.memset(r_row[0:1, :], 0.0), inc="dve")
            p.wait("tensor", "dve", rz)
            for i in range(8):
                tt = seg * 8 + i
                pcs = slice(1024 + (tt % 2) * 512, 1024 + (tt % 2) * 512 + 8)
                ptt = slice(1024 + (tt % 2) * 512, 1024 + (tt % 2) * 512 + 8)
                if tt >= 2:
                    p.wait("tensor", "dve", cum_evt[tt - 2])
                p.op("tensor", lambda e, pcs=pcs, tt=tt, i=i: e.matmul(
                    psA[:, pcs], triu_f[:, :], fst_b[:, tt, :], start=True, stop=(i == 0)))
                if i > 0:
                    p.op("tensor", lambda e, pcs=pcs: e.matmul(
                        psA[:, pcs], ones_row_f[0:1, :], r_row[0:1, :], start=False, stop=True))
                tot = p.op("tensor", lambda e, ptt=ptt, tt=tt: e.matmul(
                    psB[0:1, ptt], ones_col_f[:, :], fst_b[:, tt, :],
                    start=True, stop=True), inc="pe")
                p.wait("vector", "pe", tot)
                v = p.op("vector", lambda e, pcs=pcs, tt=tt: e.tensor_copy(cums[:, tt, :], psA[:, pcs]), inc="dve")
                p.wait("vector", "dve", v)
                cum_evt[tt] = p.op("vector", lambda e, ptt=ptt: e.tensor_tensor(
                    r_row[0:1, :], r_row[0:1, :], psB[0:1, ptt], op=ALU.add), inc="dve")
                p.wait("vector", "dve", cum_evt[tt])
                p.wait("tensor", "dve", cum_evt[tt])
            v = p.op("vector", lambda e, seg=seg: e.tensor_copy(
                cnt_row[0:1, seg * 8:(seg + 1) * 8], r_row[0:1, :]), inc="dve")
            p.wait("vector", "dve", v)
        p.wait("gpsimd", "dve", v)
        d = p.dma("gpsimd", CNTl[idx][:, :].rearrange("s e -> (s e)").unsqueeze(0), cnt_row[0:1, :], "cnt")
        p.dwait("gpsimd", "cnt")
        p.op("gpsimd", lambda e: e.collective_compute(
            "AllGather", ALU.bypass, replica_groups=[list(range(NCORE))],
            ins=[CNTl[idx]], outs=[CNTg[idx]]), inc="cc")
        p.wait("gpsimd", "cc")
        d = p.dma("gpsimd", cg_s[:, :], CNTg[idx][:, :], "cgl")
        p.dwait("tensor", "cgl")
        mmdone = p.op("tensor", lambda e: e.matmul(psB[0:2, 0:8], selmat[:, :], cg_s[:, :],
                                                   start=True, stop=True), inc="pe")
        p.wait("vector", "pe", mmdone)
        v = p.op("vector", lambda e: e.tensor_copy(offs_v[:, :], psB[0:2, 0:8]), inc="dve")
        p.wait("gpsimd", "dve", v)
        d = p.dma("gpsimd", ROWS[4:6, 0:8], offs_v[:, :], "lnr")
        p.dwait("gpsimd", "lnr")
        for seg in range(2):
            p.dma("gpsimd", offb[:, seg, :], ROWS[4 + seg:5 + seg, 0:8].partition_broadcast(128), "lna")
        p.dwait("vector", "lna")
        for seg in range(2):
            v = p.op("vector", lambda e, seg=seg: e.tensor_tensor(
                g_u[:, ts(seg, 8), :], cums[:, ts(seg, 8), :],
                offb[:, seg:seg + 1, :].broadcast_to([128, 8, 8]), op=ALU.add), inc="dve")
        p.wait("vector", "dve")
        v = p.op("vector", lambda e: e.tensor_tensor(g_u[:, :, :], g_u[:, :, :], fst_b[:, :, :],
                                                     op=ALU.mult), inc="dve")
        p.wait("vector", "dve", v)
        v = p.op("vector", lambda e: e.reduce_sum(g_loc[:, :], g_u[:, :, :], axis=AX.X), inc="dve")
        p.wait("vector", "dve", v)
        if tappfx + "dbg" in taps:
            tdb = tap(tappfx + "dbg", (130, 16))
            p.wait("gpsimd", "dve")
            p.dma("gpsimd", tdb[0:128, :], g_loc[:, :], "tap")
            p.dma("gpsimd", tdb[128:129, :], cnt_row[0:1, 0:16], "tap")
            p.dwait("gpsimd", "tap")
        if tappfx + "cg" in taps:
            tcg = tap(tappfx + "cg", (16, 8))
            p.wait("gpsimd", "dve")
            p.dma("gpsimd", tcg[:, :], cg_s[:, :], "tap")
            p.dwait("gpsimd", "tap")
        v = p.op("vector", lambda e: e.tensor_scalar(g_keep[:, :], g_loc[:, :], CAP + 0.5, None,
                                                     op0=ALU.is_lt), inc="dve")
        p.wait("vector", "dve", v)
        v = p.op("vector", lambda e: e.tensor_mul(g_gm[:, :], g_keep[:, :], g_gv[:, :]), inc="dve")
        p.wait("vector", "dve", v)
        v = p.op("vector", lambda e: e.tensor_tensor(
            g_w[:, :, :], fst_b[:, :, :], g_gm[:, :].unsqueeze(2).broadcast_to([128, 16, 8]),
            op=ALU.mult), inc="dve")
        p.wait("tensor", "dve", v)
        gmr_evt = {}
        for tt in range(16):
            par = tt % 2
            if tt >= 2:
                p.wait("tensor", "dve", gmr_evt[tt - 2])
            mmdone = p.op("tensor", lambda e, tt=tt, par=par: e.transpose(
                psA[0:8, 1024 + par * 512: 1024 + par * 512 + 128], g_w[:, tt, :], id_f[:, :]), inc="pe")
            p.wait("vector", "pe", mmdone)
            gmr_evt[tt] = p.op("vector", lambda e, tt=tt, par=par: e.tensor_copy(
                gmr[0:8, ts(tt, 128)], psA[0:8, 1024 + par * 512: 1024 + par * 512 + 128]), inc="dve")
        p.wait("gpsimd", "dve")
        d = p.dma("gpsimd", GM[idx][:, :], gmr[0:8, :], "gms")
        p.dwait("gpsimd", "gms")
        p.wait("gpsimd", "cc", CC_W)
        for e_ in range(E_):
            p.dma("gpsimd", cmoe[0:1, e_ * 1024:(e_ + 1) * 1024],
                  W_AG[e_ * WROW + KT * KT * 128: e_ * WROW + KT * KT * 128 + KT, :]
                  .rearrange("dt m -> (dt m)").unsqueeze(0), "cmo")
        if tappfx + "gm" in taps:
            tpg = tap(tappfx + "gm", (E_, T))
            p.dma("gpsimd", tpg[:, :], GM[idx][:, :], "tap")
            p.dwait("gpsimd", "tap")
        p.dwait("tensor", "cmo")

        p.wait("sync", "cc", CC_W)
        p.wait("sync", "dve")
        p.wait("sync", "act")
        p.wait("vector", "pe")
        p.wait("tensor", "act")
        p.wait("tensor", "dve")
        moe_evt = {}
        mm_evt = {}
        zprev = None
        for ch in range(4):
            zdone = None
            if ch > 0:
                p.wait("vector", "pe", mm_evt[ch * KT - 1])
                p.wait("gpsimd", "pe", mm_evt[ch * KT - 1])
                p.wait("gpsimd", "dve", zprev)
            gmd = None
            for e_ in range(E_):
                gmd = p.dma("gpsimd", gm_ch[:, e_, :],
                            GM[idx][e_:e_ + 1, ts(ch, 512)].partition_broadcast(128), "gmc")
            p.dwait("vector", "gmc")
            for e_ in range(E_):
                zdone = p.op("vector", lambda e, e_=e_, ch=ch: e.tensor_tensor(
                    zbuf[:, e_, :, :], arena_A[:, :, ts(ch, 512)],
                    gm_ch[:, e_, :].unsqueeze(1).broadcast_to([128, KT, 512]),
                    op=ALU.mult), inc="dve")
            zprev = zdone
            p.wait("tensor", "dve", zdone)
            p.dwait("tensor", "gmc")
            for dt in range(KT):
                par = dt % 2
                gidx = ch * KT + dt
                wm = wmoe_a if par == 0 else wmoe_b
                if gidx >= 2:
                    p.wait("sync", "pe", mm_evt[gidx - 2])
                for e8 in range(NCORE):
                    d = p.dma("sync", wm[:, e8, :, :],
                              W_AG[e8 * WROW + dt * KT * 128: e8 * WROW + (dt + 1) * KT * 128, :]
                              .rearrange("(kt p) m -> p kt m", p=128), "wm" + str(par))
                p.dwait("tensor", "wm" + str(par))
                if gidx >= 2:
                    p.wait("tensor", "dve", moe_evt[gidx - 2])
                ps = psA if par == 0 else psB
                for e_ in range(E_):
                    for kt in range(KT):
                        p.op("tensor", lambda e, ps=ps, wm=wm, e_=e_, kt=kt:
                             e.matmul(ps[:, 0:512], wm[:, e_, kt, :], zbuf[:, e_, kt, :],
                                      start=(e_ == 0 and kt == 0), stop=False))
                    mmdone = p.op("tensor", lambda e, ps=ps, e_=e_, dt=dt:
                         e.matmul(ps[:, 0:512], cmoe[0:1, (e_ * KT + dt) * 128:(e_ * KT + dt) * 128 + 128],
                                  gm_ch[0:1, e_, :],
                                  start=False, stop=(e_ == E_ - 1)),
                         inc="pe" if e_ == E_ - 1 else None)
                mm_evt[gidx] = mmdone
                p.wait("vector", "pe", mmdone)
                moe_evt[gidx] = p.op("vector", lambda e, ps=ps, dt=dt, ch=ch: e.tensor_add(
                    xbf[:, dt, ts(ch, 512)], ps[:, 0:512], arena_A[:, dt, ts(ch, 512)]), inc="dve")
        if tappfx + "y2pre" in taps:
            tp = tap(tappfx + "y2pre", (D_, T), BF16)
            p.wait("gpsimd", "dve")
            p.dma("gpsimd", tp[:, :].rearrange("(kt p) t -> p kt t", p=128), xbf[:, :, :], "tap")
            p.dwait("vector", "tap")
            p.dwait("scalar", "tap")
        emit_ln(arena_A, gain=bias["g2"], beta=bias["be2"])

    def emit_residual(src_dram, dst_dram, scale, tapnm=None):
        use_evt, st_evt, cp_evt = {}, {}, {}
        p.dwait("scalar", "x12")
        for kt in range(KT):
            par = kt % 2
            if kt >= 2:
                p.wait("scalar", "dve", use_evt[kt - 2])
            d = p.dma("scalar", xr_l[:, par, :], src_dram[ts(kt, 128), :], "xr" + str(par))
            p.dwait("vector", "xr" + str(par))
            if kt >= 1:
                p.dwait("vector", "x12")
                p.wait("vector", "act", cp_evt[kt - 1])
            use_evt[kt] = p.op("vector", lambda e, kt=kt, par=par, scale=scale: e.scalar_tensor_tensor(
                out=s1[:, :], in0=arena_A[:, kt, :], scalar=scale, in1=xr_l[:, par, :],
                op0=ALU.mult, op1=ALU.add), inc="dve")
            p.wait("sync", "dve", use_evt[kt])
            p.wait("scalar", "dve", use_evt[kt])
            st_evt[kt] = p.dma("sync", dst_dram[ts(kt, 128), :], s1[:, :], "x12")
            cp_evt[kt] = p.op("scalar", lambda e, kt=kt: e.activation(
                out=xbf[:, kt, :], in_=s1[:, :], func=AF.Copy), inc="act")
        p.dwait("vector", "x12")
        if tapnm is not None:
            tp = tap(tapnm, (D_, T))
            if tp is not None:
                p.dwait("gpsimd", "x12")
                p.dma("gpsimd", tp[:, :], dst_dram[:, :], "tap")
                p.dwait("scalar", "tap")
                p.dwait("vector", "tap")
                p.dwait("sync", "tap")

    if not _stop("ln0"):
        d = p.dma("gpsimd", xbf[:, :, :], xT.rearrange("(kt p) t -> p kt t", p=128), "xb0")
        p.dwait("vector", "xb0")
        p.dwait("tensor", "xb0")
        p.barrier()
        emit_ln(arena_A, tapnm="xin0")
        p.barrier()
    if not _stop("ffn1"):
        emit_ffn(0, tappfx="f1_")
        p.barrier()
    if not _stop("res1"):
        emit_residual(xT, XR1, 0.5, tapnm="x1")
        p.barrier()
    if not _stop("ain"):
        emit_ln(arena_A, tapnm="ain")
        p.barrier()

    do_qkv = not _stop("qkv")
    do_attn = not _stop("attn")
    do_wo = not _stop("wo")
    WV_D = p.dma("gpsimd", wv_s[:, :, :], Wv_nat.rearrange("(kt p) n -> p kt n", p=128), "wv") if do_qkv else 0

    def q_evac(dt, ps, par):
        return p.op("scalar", lambda e, ps=ps, dt=dt: e.activation(
            out=att_q[:, dt, :], in_=ps[:, :], func=AF.Identity, bias=bias["bq"][:, dt:dt + 1],
            scale=0.125), inc="act")
    if do_qkv:
        emit_proj(Wq_t, KT, arena_A, q_evac)

    def k_evac(dt, ps, par):
        a = p.op("scalar", lambda e, ps=ps, dt=dt, par=par: e.activation(
            out=evtmp[:, par, :], in_=ps[:, :], func=AF.Identity, bias=bias["bk"][:, dt:dt + 1]), inc="act")
        p.wait("sync", "act", a)
        p.dma("sync", K_loc[ts(dt, 128), :], evtmp[:, par, :], "kls")
        p.dwait("scalar", "kls")
        return a
    if do_qkv:
      emit_proj(Wk_t, KT, arena_A, k_evac)

      p.dwait("tensor", "wv")
      p.wait("tensor", "act")
      p.wait("tensor", "dve")
      p.wait("gpsimd", "pe", p.at("pe"))
      p.dma("gpsimd", ROWS[6:7, 0:1024].rearrange("r (h v) -> (r h) v", v=64),
            bv_h.rearrange("v h -> h v"), "bvb", allow_slow_non_contiguous=True)
      p.dwait("gpsimd", "bvb")
      p.dma("gpsimd", bvbc[:, :], ROWS[6:7, 0:1024].partition_broadcast(128), "bvb")
      p.dwait("vector", "bvb")
      p.dwait("gpsimd", "kls")
      for c in range(4):
          p.op("gpsimd", lambda e, c=c: e.collective_compute(
              "AllGather", ALU.bypass, replica_groups=[list(range(NCORE))],
              ins=[K_loc[c * 256:(c + 1) * 256, :]], outs=[K_AGc[c]]), inc="cc")
      p.dwait("vector", "kls")
      v_evt = {}
      vcnt = 0
      for tt in range(16):
          for nch in range(2):
              par = vcnt % 2
              if vcnt >= 2:
                  p.dwait("tensor", "vls")
              ps = psA if par == 0 else psB
              for kt in range(KT):
                  mmdone = p.op("tensor", lambda e, ps=ps, tt=tt, kt=kt, nch=nch:
                       e.matmul(ps[:, 0:512], arena_A[:, kt, ts(tt, 128)], wv_s[:, kt, ts(nch, 512)],
                                start=(kt == 0), stop=(kt == KT - 1)),
                       inc="pe" if kt == KT - 1 else None)
              p.wait("vector", "pe", mmdone)
              a = p.op("vector", lambda e, ps=ps, par=par, nch=nch: e.tensor_add(
                  evtmp[:, par, 0:512], ps[:, 0:512], bvbc[:, ts(nch, 512)]), inc="dve")
              p.wait("sync", "dve", a)
              for hh in range(8):
                  h = nch * 8 + hh
                  v_evt[vcnt] = p.dma("sync", V_loc[h * T + tt * 128: h * T + tt * 128 + 128, 0:64],
                                      evtmp[:, par, hh * 64:(hh + 1) * 64], "vls")
              vcnt += 1
      p.dwait("gpsimd", "vls")
      cc_chunk = {}
      for c in range(4):
          p.op("gpsimd", lambda e, c=c: e.collective_compute(
              "AllGather", ALU.bypass, replica_groups=[list(range(NCORE))],
              ins=[V_loc[c * 4 * T:(c + 1) * 4 * T, :]], outs=[V_AGc[c]]), inc="cc")
          cc_chunk[c] = p.at("cc")
    if not do_qkv:
        cc_chunk = {c: 0 for c in range(4)}
    p.barrier()

    ev_mm, ev_exp, ev_tt, ev_pv, ev_nrm = {}, {}, {}, {}, {}
    nrm_fin = {}
    pending_nrm = []
    pending_b = []

    ev_rcp = {}
    ot_last = {}

    def emit_nrm_a(jp, qc):
        p.wait("gpsimd", "dve", ev_nrm[(jp, qc)])
        p.dma("gpsimd", ROWS[6:7, 0:2048], pvo2[64:65, :], "nr")
        prevr = ev_rcp.get((jp, qc - 1)) if qc > 0 else (ev_rcp.get((jp - 1, 1)) if jp > 0 else None)
        if prevr is not None:
            p.wait("gpsimd", "dve", prevr)
        p.dwait("gpsimd", "nr")
        p.dma("gpsimd", dsm[:, :], ROWS[6:7, 0:2048].rearrange("r (p f) -> (r p) f", p=128), "nr")

    def emit_nrm_finish(jp, qc):
        p.dwait("vector", "nr")
        evr = p.op("vector", lambda e: e.reciprocal(dsm2[:, :], dsm[:, :]), inc="dve")
        ev_rcp[(jp, qc)] = evr
        p.wait("gpsimd", "dve", evr)
        p.dma("gpsimd", ROWS[7:8, 0:2048].rearrange("r (p f) -> (r p) f", p=128), dsm2[:, :], "nr")
        p.dwait("gpsimd", "nr")
        p.dma("gpsimd", rcpb[:, :], ROWS[7:8, 0:2048].partition_broadcast(128), "nb")
        p.dwait("gpsimd", "nb")
        p.dwait("gpsimd", "ot0")
        mv = p.op("gpsimd", lambda e: e.tensor_mul(mvb[:, :], pvo2[0:64, :], rcpb[0:64, :]), inc="pool")
        nrm_fin[(jp, qc)] = mv
        p.wait("gpsimd", "pool", mv)
        for bh in range(4):
            b, hh = bh // 2, bh % 2
            h = jp * 2 + hh
            p.dma("gpsimd", OT[h * 64:(h + 1) * 64, b * SLOC + qc * 512: b * SLOC + qc * 512 + 512],
                  mvb[:, bh * 512:(bh + 1) * 512], "ot0")
    mk_evt = {}
    tt_hist = {}
    pv_last_of_jp = {}

    def emit_pv(jp, qc, kt, b, par):
        key = (jp, qc, kt, b)
        p.wait("tensor", "dve", ev_tt[key])
        if kt == 0:
            prevq = ev_nrm.get((jp, qc - 1)) if qc > 0 else (ev_nrm.get((jp - 1, 1)) if jp > 0 else None)
            if prevq is not None:
                p.wait("tensor", "dve", prevq)
        for hh in range(2):
            bh = b * 2 + hh
            fn = lambda e, kt=kt, b=b, hh=hh, bh=bh, par=par: e.matmul(
                psB[0:65, ts(bh, 512)], vau[:, par, bh, kt * 65: kt * 65 + 65],
                pmb[:, kt % 2, b, hh * 512:(hh + 1) * 512],
                start=(kt == 0), stop=(kt == 63))
            if hh == 1:
                ev_pv[key] = p.op("tensor", fn, inc="pe")
            else:
                p.op("tensor", fn)

    def kv_issue(jpx):
        pr = jpx % 2
        c, jo = jpx // 2, jpx % 2
        if jpx >= 2:
            p.wait("gpsimd", "pe", pv_last_of_jp[jpx - 2])
        p.wait("gpsimd", "cc", cc_chunk[c])
        for b in range(2):
            for r in range(NCORE):
                p.dma("gpsimd", kst[:, pr, b, ts(r, 1024)],
                      K_AGc[c][r * 256 + jo * 128: r * 256 + jo * 128 + 128, ts(b, 1024)], "kv" + str(pr))
        for bh in range(4):
            b, hh = bh // 2, bh % 2
            hl = jo * 2 + hh
            for r in range(NCORE):
                p.dma("gpsimd", vau[:, pr, bh, :].rearrange("p (kt c) -> p kt c", c=65)[:, r * 8:(r + 1) * 8, :],
                      V_AGc[c][(r * 4 + hl) * T + b * SLOC: (r * 4 + hl) * T + b * SLOC + SLOC, :]
                      .rearrange("(kt p) c -> p kt c", p=128), "kv" + str(pr))
    if do_attn:
        kv_issue(0)
    for jp in range(KT if do_attn else 0):
        par = jp % 2
        p.dwait("tensor", "kv" + str(par))
        for qc in range(2):
            def emit_stage(skt):
                if skt % 2 == 0:
                    g = skt // 2
                    gpar = g % 2
                    gi = (jp * 2 + qc) * 32 + g
                    if gi >= 2 and (gi - 2) in tt_hist:
                        p.wait("sync", "dve", tt_hist[gi - 2])
                    mk_evt[(jp, qc, g)] = p.dma(
                        "sync", mkb[:, gpar, :, :],
                        maskT[g * 256:(g + 1) * 256, qc * 512: qc * 512 + 512]
                        .rearrange("(t p) c -> p t c", p=128), "mk" + str(gpar))
                for b in range(2):
                    key = (jp, qc, skt, b)
                    ps_sc = psA[:, b * 1024:(b + 1) * 1024]
                    if skt >= 1:
                        p.wait("tensor", "act", ev_exp[(jp, qc, skt - 1, b)])
                    q0 = att_q[:, jp, b * SLOC + qc * 512: b * SLOC + qc * 512 + 512]
                    ksl = kst[:, par, b, skt * 128: skt * 128 + 128]
                    p.op("tensor", lambda e, ps_sc=ps_sc, ksl=ksl, q0=q0:
                         e.matmul(ps_sc[:, 0:512], ksl[0:64, :], q0[0:64, :],
                                  start=True, stop=True, tile_position=(0, 0)))
                    ev_mm[key] = p.op("tensor", lambda e, ps_sc=ps_sc, ksl=ksl, q0=q0:
                         e.matmul(ps_sc[:, 512:1024], ksl[64:128, :], q0[64:128, :],
                                  start=True, stop=True, tile_position=(64, 0)), inc="pe")
                    p.wait("scalar", "pe", ev_mm[key])
                    prev_tt2 = ev_tt.get((jp, qc, skt - 2, b))
                    if prev_tt2 is not None:
                        p.wait("scalar", "dve", prev_tt2)
                    ev_exp[key] = p.op("scalar", lambda e, ps_sc=ps_sc, skt=skt, b=b: e.activation(
                        out=peb[:, skt % 2, b, :], in_=ps_sc[:, :], func=AF.Exp), inc="act")
                    p.wait("vector", "act", ev_exp[key])
                    p.dwait("vector", "mk" + str((skt // 2) % 2))
                    prev_pv = ev_pv.get((jp, qc, skt - 2, b))
                    if prev_pv is not None:
                        p.wait("vector", "pe", prev_pv)
                    ev_tt[key] = p.op("vector", lambda e, skt=skt, b=b: e.tensor_tensor(
                        pmb[:, skt % 2, b, :].rearrange("p (two n) -> p two n", two=2),
                        peb[:, skt % 2, b, :].rearrange("p (two n) -> p two n", two=2),
                        mkb[:, (skt // 2) % 2, skt % 2, :].unsqueeze(1).broadcast_to([128, 2, 512]),
                        op=ALU.mult), inc="dve")
                    if b == 1:
                        tt_hist[(jp * 2 + qc) * 32 + skt // 2] = ev_tt[key]
            emit_stage(0)
            for kt in range(64):
                if kt == 4 and pending_nrm:
                    pending_b.append(pending_nrm.pop(0))
                    emit_nrm_a(*pending_b[-1])
                if kt == 12 and pending_b:
                    emit_nrm_finish(*pending_b.pop(0))
                if kt == 16 and qc == 0 and jp + 1 < KT:
                    kv_issue(jp + 1)
                if kt + 1 < 64:
                    emit_stage(kt + 1)
                emit_pv(jp, qc, kt, 0, par)
                emit_pv(jp, qc, kt, 1, par)
            p.wait("vector", "pe", ev_pv[(jp, qc, 63, 0)])
            p.wait("vector", "pe", ev_pv[(jp, qc, 63, 1)])
            prevf = nrm_fin.get((jp, qc - 1)) if qc > 0 else (nrm_fin.get((jp - 1, 1)) if jp > 0 else None)
            if prevf is not None:
                p.wait("vector", "pool", prevf)
            cpv = None
            for half2 in range(2):
                cpv = p.op("vector", lambda e, half2=half2: e.tensor_copy(
                    pvo2[:, half2 * 1024:(half2 + 1) * 1024],
                    psB[0:65, half2 * 1024: half2 * 1024 + 1024]), inc="dve")
            ev_nrm[(jp, qc)] = cpv
            pending_nrm.append((jp, qc))
        pv_last_of_jp[jp] = p.at("pe")
    while pending_nrm:
        pending_b.append(pending_nrm.pop(0))
        emit_nrm_a(*pending_b[-1])
        emit_nrm_finish(*pending_b.pop(0))
    p.barrier()

    if do_wo:
        p.dwait("gpsimd", "ot0")
        p.dwait("gpsimd", "ot1")
        d = p.dma("gpsimd", arena_A[:, :, :], OT[:, :].rearrange("(kt p) t -> p kt t", p=128), "oto")
        p.dwait("tensor", "oto")
        p.dwait("scalar", "x12")
    res_use, res_st, res_cp = {}, {}, {}

    def wo_evac(dt, ps, par):
        a = p.op("scalar", lambda e, ps=ps, dt=dt, par=par: e.activation(
            out=evtmp[:, par, :], in_=ps[:, :], func=AF.Identity, bias=bias["bo"][:, dt:dt + 1]), inc="act")
        if dt >= 2:
            p.wait("scalar", "dve", res_use[dt - 2])
        dl = p.dma("scalar", xr_l[:, par, :], XR1[ts(dt, 128), :], "xr" + str(par))
        p.wait("vector", "act", a)
        p.dwait("vector", "xr" + str(par))
        if dt >= 1:
            p.dwait("vector", "x12")
            p.wait("vector", "act", res_cp[dt - 1])
        res_use[dt] = p.op("vector", lambda e, par=par, dt=dt: e.tensor_add(
            s1[:, :], evtmp[:, par, :], xr_l[:, par, :]), inc="dve")
        p.wait("sync", "dve", res_use[dt])
        p.wait("scalar", "dve", res_use[dt])
        res_st[dt] = p.dma("sync", XR2[ts(dt, 128), :], s1[:, :], "x12")
        res_cp[dt] = p.op("scalar", lambda e, dt=dt: e.activation(
            out=xbf[:, dt, :], in_=s1[:, :], func=AF.Copy), inc="act")
        return a
    if do_wo:
        emit_proj(Wo_t, KT, arena_A, wo_evac)
        p.dwait("vector", "x12")
        p.dwait("scalar", "x12")
        if "x2" in taps:
            tp = tap("x2", (D_, T))
            p.dwait("gpsimd", "x12")
            p.dma("gpsimd", tp[:, :], XR2[:, :], "tap")
            p.dwait("scalar", "tap")
            p.dwait("vector", "tap")
        p.barrier()
    if not _stop("lnx2"):
        emit_ln(arena_A)
        p.barrier()
    if not _stop("ffn2"):
        emit_ffn(1, tappfx="f2_")
        p.barrier()
    if not _stop("final"):
        emit_residual(XR2, outT, 0.5)
    for e in ENGS:
        p.dwait(e, "x12")
        p.dwait(e, "tap")
    p.barrier()

    with p.es:
        with nc.Block() as block:
            p.build(block)
    return nc, sorted(tap_t.keys())



def _tile_w(w):
    din, dout = w.shape
    ktn, dtn = din // 128, dout // 128
    return np.ascontiguousarray(
        w.reshape(ktn, 128, dtn, 128).transpose(2, 0, 1, 3).reshape(dtn * ktn * 128, 128))


def _col(v):
    return np.ascontiguousarray(v.reshape(-1, 128).T)


def _build_mask(c):
    import ml_dtypes
    i = np.arange(S_)
    xi = (i // GRID_) % GRID_
    yi = i % GRID_
    dxa = np.abs(xi[:, None] - xi[None, :])
    dya = np.abs(yi[:, None] - yi[None, :])
    dx = np.minimum(dxa, GRID_ - dxa).astype(np.float32)
    dy = np.minimum(dya, GRID_ - dya).astype(np.float32)
    dist = np.sqrt(dx * dx + dy * dy)
    m = np.exp(-np.float32(DECAY_) * dist).astype(np.float32)
    qi = np.arange(c * SLOC, (c + 1) * SLOC)
    sub = m[qi, :]
    tril = (np.arange(S_)[None, :] <= qi[:, None])
    mm = sub * tril + np.float32(1e-10)
    return np.ascontiguousarray(mm.T.astype(ml_dtypes.bfloat16))


def make_in_maps(inputs):
    import ml_dtypes
    f = np.float32
    x = np.asarray(inputs["L0_attn_hidden_states"], f)
    Wq = np.asarray(inputs["L0_attn_Wq"], f); bq = np.asarray(inputs["L0_attn_bq"], f)
    Wk = np.asarray(inputs["L0_attn_Wk"], f); bk = np.asarray(inputs["L0_attn_bk"], f)
    Wv = np.asarray(inputs["L0_attn_Wv"], f); bv = np.asarray(inputs["L0_attn_bv"], f)
    Wo = np.asarray(inputs["L0_attn_Wo"], f); bo = np.asarray(inputs["L0_attn_bo"], f)
    w_in = np.asarray(inputs["L0_ffn_in_proj_w"], f); b_in = np.asarray(inputs["L0_ffn_in_proj_b"], f)
    w_out = np.asarray(inputs["L0_ffn_out_proj_w"], f); b_out = np.asarray(inputs["L0_ffn_out_proj_b"], f)
    g1 = np.asarray(inputs["L0_ffn_ln1_g"], f); be1 = np.asarray(inputs["L0_ffn_ln1_b"], f)
    g2 = np.asarray(inputs["L0_ffn_ln2_g"], f); be2 = np.asarray(inputs["L0_ffn_ln2_b"], f)
    wg = np.asarray(inputs["L0_ffn_wg"], f)
    w1 = np.asarray(inputs["L0_ffn_w1"], f); b1 = np.asarray(inputs["L0_ffn_b1"], f)
    w2 = np.asarray(inputs["L0_ffn_w2"], f); b2 = np.asarray(inputs["L0_ffn_b2"], f)

    in_maps = []
    for c in range(NCORE):
        xc = x[:, c * SLOC:(c + 1) * SLOC, :].reshape(T, D_).T
        sel = np.zeros((16, 2), np.float32)
        for r in range(NCORE):
            if r < c:
                sel[r * 2 + 0, 0] = 1.0
                sel[r * 2 + 1, 1] = 1.0
            sel[r * 2 + 0, 1] = 1.0
        von = np.zeros((NH * T, 65), ml_dtypes.bfloat16)
        von[:, 64] = 1.0
        im = {
            "xT": np.ascontiguousarray(xc),
            "Wq_t": _tile_w(Wq), "Wk_t": _tile_w(Wk), "Wo_t": _tile_w(Wo),
            "Wv_nat": np.ascontiguousarray(Wv),
            "bq_c": _col(bq * 0.125), "bk_c": _col(bk), "bo_c": _col(bo),
            "bv_h": np.ascontiguousarray(bv.reshape(NH, 64).T),
            "Win_t": _tile_w(w_in.T), "bin_c": _col(b_in),
            "Wout_t": _tile_w(w_out.T), "bout_c": _col(b_out),
            "g1_c": _col(g1), "be1_c": _col(be1), "g2_c": _col(g2), "be2_c": _col(be2),
            "wg_in": np.ascontiguousarray(wg),
            "w1T_t": np.ascontiguousarray(
                w1[c].T.reshape(32, 128, KT, 128).transpose(2, 0, 1, 3).reshape(KT * 32 * 128, 128)),
            "w2_nat": np.ascontiguousarray(w2[c]),
            "b1_c": _col(b1[c]),
            "b2_row": np.ascontiguousarray(b2[c].reshape(1, D_)),
            "maskT": _build_mask(c),
            "gate_r": np.broadcast_to(8.0 - np.arange(E_, dtype=np.float32), (128, E_)).copy(),
            "moe_sel": sel,
            "vones": von,
        }
        in_maps.append(im)
    return in_maps


def kernel(**inputs):
    nc, _ = build_nc()
    in_maps = make_in_maps(inputs)
    res = run_bass_kernel_spmd(nc, in_maps, core_ids=list(range(NCORE)))
    out = np.empty((B_, S_, D_), np.float32)
    for c in range(NCORE):
        o = res.results[c]["outT"]
        out[0, c * SLOC:(c + 1) * SLOC, :] = o[:, 0:SLOC].T
        out[1, c * SLOC:(c + 1) * SLOC, :] = o[:, SLOC:T].T
    return out
